# version 03 (final)
# baseline (speedup 1.0000x reference)
import numpy as np
import ml_dtypes
import concourse.bass as bass
import concourse.mybir as mybir
from concourse.bass_utils import run_bass_kernel_spmd

F32 = mybir.dt.float32
BF16 = mybir.dt.bfloat16
AF = mybir.ActivationFunctionType
ALU = mybir.AluOpType
AX = mybir.AxisListType

D = 1024
T = 4096
NH = 16
HD = 64
PM = 64
LN2_8 = 2.0794415416798357

ENGINES = ("tensor", "scalar", "vector", "gpsimd", "sync")


class Prog:

    def __init__(self, name="mk"):
        self.nc = bass.Bass("TRN2", target_bir_lowering=False)
        self.ops = {e: [] for e in ENGINES}
        self.sig_counts = {e: 0 for e in ENGINES}
        self.dma_counts = {e: 0 for e in ENGINES}
        self._wait_acc = {e: [] for e in ENGINES}

    def op(self, engine, fn):
        rec = {"fn": fn, "waits": self._wait_acc[engine], "sig": None}
        self._wait_acc[engine] = []
        self.ops[engine].append(rec)
        return rec

    def sig(self, engine):
        rec = self.ops[engine][-1]
        if rec["sig"] is not None:
            return rec["sig"]
        self.sig_counts[engine] += 1
        rec["sig"] = ("eng", engine, self.sig_counts[engine])
        return rec["sig"]

    def op_sig(self, engine, fn):
        self.op(engine, fn)
        return self.sig(engine)

    def opw(self, engine, fn):
        if self.ops[engine]:
            prev = self.ops[engine][-1]
            if prev["sig"] is None:
                self.sig(engine)
            self.wait(engine, prev["sig"])
        self.op(engine, fn)
        return self.sig(engine)

    def dma(self, queue, out, in_, **kw):
        self.dma_counts[queue] += 16
        rec = {
            "fn": (lambda eng, out=out, in_=in_, kw=kw: eng.dma_start(out, in_, **kw)),
            "waits": self._wait_acc[queue],
            "sig": ("dma", queue, self.dma_counts[queue]),
        }
        self._wait_acc[queue] = []
        self.ops[queue].append(rec)
        return rec["sig"]

    def wait(self, engine, *evs):
        for ev in evs:
            if ev is not None:
                self._wait_acc[engine].append(ev)

    def barrier(self):
        evs = []
        for e in ENGINES:
            if self.ops[e]:
                rec = self.ops[e][-1]
                if rec["sig"] is not None:
                    evs.append(rec["sig"])
                else:
                    evs.append(self.sig(e))
        for e in ENGINES:
            for ev in evs:
                if not (ev[0] == "eng" and ev[1] == e):
                    self.wait(e, ev)
        return evs

    def build(self):
        nc = self.nc
        eng_sems = {}
        dma_sems = {}
        for e in ENGINES:
            if self.sig_counts[e] > 0:
                eng_sems[e] = nc.semaphore(f"se_{e}").__enter__()
            if self.dma_counts[e] > 0:
                dma_sems[e] = nc.semaphore(f"sd_{e}").__enter__()

        def sem_of(ev):
            kind, e, val = ev
            return (eng_sems[e] if kind == "eng" else dma_sems[e]), val

        block = nc.Block().__enter__()
        for e in ENGINES:
            if not self.ops[e]:
                continue
            oplist = self.ops[e]
            tail_waits = self._wait_acc[e]

            def body(eng, oplist=oplist, e=e, tail_waits=tail_waits):
                max_waited = {}

                def emit_waits(waits):
                    best = {}
                    for ev in waits:
                        sem, val = sem_of(ev)
                        key = id(sem)
                        if val > best.get(key, (None, -1))[1]:
                            best[key] = (sem, val)
                    for key, (sem, val) in best.items():
                        if val > max_waited.get(key, -1):
                            eng.wait_ge(sem, val)
                            max_waited[key] = val

                for rec in oplist:
                    emit_waits(rec["waits"])
                    ins = rec["fn"](eng)
                    if rec["sig"] is not None:
                        kind, ee, val = rec["sig"]
                        sem = eng_sems[ee] if kind == "eng" else dma_sems[ee]
                        ins.then_inc(sem, 16 if kind == "dma" else 1)
                emit_waits(tail_waits)

            getattr(block, e)(body)
        block.__exit__(None, None, None)
        return nc



def bias_cols(b):
    F = b.shape[0]
    return np.ascontiguousarray(b.reshape(F // 128, 128).T)


def kmajor(w):
    return np.ascontiguousarray(w)


def host_prep(inputs):
    g = {}
    g["c_ones_col"] = np.ones((128, 1), dtype=np.float32)
    g["c_ones_row"] = np.ones((1, 128), dtype=np.float32)
    g["c_neghalf_col"] = np.full((64, 1), -0.5, dtype=np.float32)
    g["c_ident"] = np.eye(128, dtype=np.float32)
    g["w_qkvT"] = np.ascontiguousarray(inputs["L0_attn_qkv_w"].T)
    g["b_qkv"] = bias_cols(inputs["L0_attn_qkv_b"])
    g["w_projT"] = np.ascontiguousarray(inputs["L0_attn_proj_w"].T)
    g["b_proj"] = bias_cols(inputs["L0_attn_proj_b"])
    pm = inputs["L0_attn_projection_matrix"]
    g["pm_rhs"] = np.ascontiguousarray(
        np.concatenate([pm.T, -np.ones((1, PM), dtype=np.float32)], axis=0))
    g["c_ident_f32"] = np.eye(128, dtype=np.float32)
    g["w_gateT"] = np.ascontiguousarray(inputs["L0_ffn_w_gate"].T)
    g["w1eT"] = np.ascontiguousarray(np.transpose(inputs["L0_ffn_w1_e"], (0, 2, 1)))
    g["w3eT"] = np.ascontiguousarray(np.transpose(inputs["L0_ffn_w3_e"], (0, 2, 1)))
    g["w2eT"] = np.ascontiguousarray(np.transpose(inputs["L0_ffn_w2_e"], (0, 2, 1)))
    g["w1sT"] = np.ascontiguousarray(inputs["L0_ffn_w1_s"].T)
    g["w3sT"] = np.ascontiguousarray(inputs["L0_ffn_w3_s"].T)
    g["w2sT"] = np.ascontiguousarray(inputs["L0_ffn_w2_s"].T)
    for nm in ["wq", "wk", "wv", "wo", "wfq", "wfg"]:
        g["l1_" + nm] = np.ascontiguousarray(inputs["L1_attn_" + nm])
    for nm in ["bq", "bk", "bv", "bo", "bfq", "bfg"]:
        g["l1_" + nm] = bias_cols(inputs["L1_attn_" + nm])
    bd = np.zeros((1024, 64), dtype=np.float32)
    for h in range(NH):
        bd[h * 64:(h + 1) * 64, h] = inputs["L1_attn_up"][h, 0, :]
        bd[h * 64:(h + 1) * 64, 32 + h] = inputs["L1_attn_uz"][h, 0, :]
    g["l1_uvdiag"] = bd
    u = np.arange(T, dtype=np.float32) - 2048.0
    g["c_iota"] = np.ascontiguousarray(np.stack([u * u, u, np.ones(T, np.float32)]))
    perm = np.concatenate([np.arange(0, 2048, 2), np.arange(1, 2048, 2)])
    g["f_w1T"] = np.ascontiguousarray(np.transpose(inputs["L1_ffn_w1"], (0, 2, 1))[:, :, perm])
    g["f_w3T"] = np.ascontiguousarray(np.transpose(inputs["L1_ffn_w3"], (0, 2, 1))[:, :, perm])
    g["f_w2T"] = np.ascontiguousarray(np.transpose(inputs["L1_ffn_w2"], (0, 2, 1)))
    g["f_b1"] = np.ascontiguousarray(inputs["L1_ffn_b1"][:, perm])
    g["f_b3"] = np.ascontiguousarray(inputs["L1_ffn_b3"][:, perm])
    g["f_b2"] = np.ascontiguousarray(inputs["L1_ffn_b2"])
    g["f_sw1T"] = np.ascontiguousarray(inputs["L1_ffn_sw1"].T[:, perm])
    g["f_sw3T"] = np.ascontiguousarray(inputs["L1_ffn_sw3"].T[:, perm])
    g["f_sw2T"] = np.ascontiguousarray(inputs["L1_ffn_sw2"].T)
    g["f_sb1"] = np.ascontiguousarray(inputs["L1_ffn_sb1"][perm][None, :])
    g["f_sb3"] = np.ascontiguousarray(inputs["L1_ffn_sb3"][perm][None, :])
    g["f_sb2"] = np.ascontiguousarray(inputs["L1_ffn_sb2"][None, :])
    g["f_gwT"] = np.ascontiguousarray(inputs["L1_ffn_gate_w"].T)
    g["f_gb"] = np.ascontiguousarray(inputs["L1_ffn_gate_b"][None, :])
    return g



def fm_view(dram_ap):
    return dram_ap.rearrange("(f p) t -> p f t", p=128)


def emit_ln(p, S, src_dram, dst_xln=None, resid_dram=None, out_dram=None, dst_xln_dram=None, tag=""):
    nc = p.nc
    src = fm_view(src_dram)
    res = fm_view(resid_dram) if resid_dram is not None else None
    out = fm_view(out_dram) if out_dram is not None else None
    TC = 512
    xc = S["ln_xc"]
    rc = S["ln_rc"]
    xb = S["ln_xb"]
    sq = S["ln_sq"]
    rows = S["ln_rows"]

    reps = S["ln_reps"]
    psS = S["PSA"]
    psR = S["PSB"]
    ones_col = S["ones_col_bf"]
    ones_row = S["ones_row_f32"]

    for tg in range(T // TC):
        sl = slice(tg * TC, (tg + 1) * TC)
        ev_x = p.dma("sync", xc[:], src[:, :, sl])
        if res is not None:
            ev_r = p.dma("sync", rc[:], res[:, :, sl])
        p.wait("vector", ev_x)
        ev_cast = p.op_sig("vector", lambda v: v.tensor_copy(xb[:], xc[:]))
        p.wait("scalar", ev_cast)
        ev_sq = p.op_sig("scalar", lambda s: s.activation(sq[:], xb[:], AF.Square))
        p.wait("tensor", ev_cast, ev_sq)
        for fc in range(8):
            p.op("tensor", lambda t, fc=fc: t.matmul(
                psS[0:1, 0:512], ones_col[:], xb[:, fc, :],
                start=(fc == 0), stop=(fc == 7)))
        ev_s1 = p.sig("tensor")
        for fc in range(8):
            p.op("tensor", lambda t, fc=fc: t.matmul(
                psS[0:1, 512:1024], ones_col[:], sq[:, fc, :],
                start=(fc == 0), stop=(fc == 7)))
        ev_s2 = p.sig("tensor")
        mean = rows[0:1, 0:512]
        qq = rows[0:1, 512:1024]
        var = rows[0:1, 1024:1536]
        tmp = rows[0:1, 1536:2048]
        rstd = rows[0:1, 2048:2560]
        ms = rows[0:1, 2560:3072]
        p.wait("vector", ev_s1, ev_s2)
        p.op("vector", lambda v: v.tensor_scalar_mul(mean, psS[0:1, 0:512], 1.0 / D))
        p.op("vector", lambda v: v.tensor_scalar_mul(qq, psS[0:1, 512:1024], 1.0 / D))
        p.op("vector", lambda v: v.tensor_tensor(tmp, mean, mean, ALU.mult))
        p.op("vector", lambda v: v.tensor_sub(var, qq, tmp))
        p.op("vector", lambda v: v.tensor_scalar_add(tmp, var, 1e-5))
        ev_var = p.sig("vector")
        p.wait("scalar", ev_var)
        p.op("scalar", lambda s: s.activation(var, tmp, AF.Ln))
        p.op("scalar", lambda s: s.activation(rstd, var, AF.Exp, scale=-0.5))
        ev_rstd = p.sig("scalar")
        p.wait("vector", ev_rstd)
        ev_ms = p.op_sig("vector", lambda v: v.tensor_tensor(ms, mean, rstd, ALU.mult))
        p.wait("tensor", ev_rstd, ev_ms)
        p.op("tensor", lambda t: t.matmul(psR[:, 0:512], ones_row[:], rstd, start=True, stop=True))
        p.op("tensor", lambda t: t.matmul(psR[:, 512:1024], ones_row[:], ms, start=True, stop=True))
        ev_rep = p.sig("tensor")
        p.wait("scalar", ev_rep)
        ev_repe = p.op_sig("scalar", lambda s: s.activation(reps[:], psR[:, 0:1024], AF.Copy))
        TT = S["ln_t"]
        p.wait("vector", ev_repe)
        if dst_xln is not None or dst_xln_dram is not None:
            for fc in range(8):
                p.op("vector", lambda v, fc=fc: v.tensor_tensor(
                    TT[:, fc, :], xc[:, fc, :], reps[:, 0:512], ALU.mult))
                if dst_xln is not None:
                    p.op("vector", lambda v, fc=fc, sl=sl: v.tensor_sub(
                        dst_xln[:, fc, sl], TT[:, fc, :], reps[:, 512:1024]))
                else:
                    p.op("vector", lambda v, fc=fc: v.tensor_sub(
                        xb[:, fc, :], TT[:, fc, :], reps[:, 512:1024]))
            ev_ap = p.sig("vector")
            if dst_xln_dram is not None:
                p.wait("sync", ev_ap)
                ev_st2 = p.dma("sync", dst_xln_dram[:, :, sl], xb[:])
                p.wait("vector", ev_st2)
            p.wait("sync", ev_ap)
            p.wait("tensor", ev_ap)
        else:
            p.wait("vector", ev_r)
            for fc in range(8):
                p.op("vector", lambda v, fc=fc: v.tensor_tensor(
                    TT[:, fc, :], xc[:, fc, :], reps[:, 0:512], ALU.mult))
                p.op("vector", lambda v, fc=fc: v.tensor_sub(
                    xc[:, fc, :], TT[:, fc, :], reps[:, 512:1024]))
                p.op("vector", lambda v, fc=fc: v.tensor_add(
                    TT[:, fc, :], xc[:, fc, :], rc[:, fc, :]))
            ev_ap = p.sig("vector")
            p.wait("sync", ev_ap)
            ev_st = p.dma("sync", out[:, :, sl], TT[:])
            p.wait("vector", ev_st)
            p.wait("tensor", ev_ap)
    return




def emit_gemm(p, S, xln, w_dram, col0, n_mc, evac_fn, mrows=128, psa_free0=None):
    ps = S["PSA"]
    wt = (S["wt0"], S["wt1"])
    ev_w = {}
    ev_mm_last = {}
    psa_free = psa_free0

    def prefetch(mc):
        if mc >= n_mc:
            return
        if mc >= 2:
            p.wait("gpsimd", ev_mm_last[mc - 2])
        ev_w[mc] = p.dma(
            "gpsimd", wt[mc % 2][:, :, 0:mrows],
            w_dram[:, col0 + mc * mrows: col0 + (mc + 1) * mrows]
            .rearrange("(k p) f -> p k f", p=128))

    prefetch(0)
    prefetch(1)
    for mc in range(n_mc):
        for th in range(2):
            p.wait("tensor", ev_w[mc], psa_free)
            for k in range(8):
                for ns in range(4):
                    p.op("tensor", lambda t, k=k, ns=ns, th=th, mc=mc: t.matmul(
                        ps[0:mrows, ns * 512:(ns + 1) * 512],
                        wt[mc % 2][:, k, 0:mrows],
                        xln[:, k, th * 2048 + ns * 512: th * 2048 + (ns + 1) * 512],
                        start=(k == 0), stop=(k == 7)))
            ev_mm = p.sig("tensor")
            if th == 1:
                ev_mm_last[mc] = ev_mm
                prefetch(mc + 2)
            psa_free = evac_fn(mc, th, ev_mm)
    return psa_free, ev_mm_last[n_mc - 1]


def emit_l0_attn(p, S, xln, dr):
    PSA, PSB = S["PSA"], S["PSB"]
    VT = S["VT"]
    ident = S["ident_bf"]
    ones_col = S["ones_col_bf"]
    ones_row = S["ones_row_f32"]

    psb_free = [None, None, None, None]

    ev_pm = p.dma("gpsimd", S["pm_bf"][:], dr["pm_rhs"])

    vtmp = S["vtmp"]

    def v_evac(vc, th, ev_mm):
        p.wait("scalar", ev_mm)
        ev_ev = p.op_sig("scalar", lambda s, vc=vc: s.activation(
            vtmp[:], PSA[:, 0:2048], AF.Identity, bias=S["b_qkv"][:, 16 + vc: 17 + vc]))
        for g in range(4):
            p.wait("tensor", ev_ev, psb_free[g])
            for j in range(4):
                tc_loc = g * 4 + j
                p.op("tensor", lambda t, g=g, j=j, tc_loc=tc_loc: t.matmul(
                    PSB[:, g * 512 + j * 128: g * 512 + (j + 1) * 128],
                    vtmp[:, tc_loc * 128:(tc_loc + 1) * 128],
                    ident[:], start=True, stop=True))
            ev_t = p.sig("tensor")
            p.wait("scalar", ev_t)
            tc0 = th * 16 + g * 4
            psb_free[g] = p.op_sig("scalar", lambda s, g=g, tc0=tc0, vc=vc: s.activation(
                VT[:, tc0:tc0 + 4, vc, :], PSB[:, g * 512:(g + 1) * 512], AF.Copy))
        return ev_ev

    emit_gemm(p, S, xln, dr["w_qkvT"], 2048, 8, v_evac)
    p.barrier()

    dn, kn = S["dn_aug"], S["kn_aug"]
    QP, KP, QPF = S["QP"], S["KP"], S["QPF"]
    kv_sb = S["kv_sb"]
    subt, etmp, mx = S["subt"], S["etmp"], S["mx"]
    nrow = S["nrow"]
    ctx_f, ctx_hb = S["ctx_f"], S["ctx_hb"]
    psa_free = None
    psb_free = [None, None, None, None]
    ev_wt_free = [None, None]
    ev_ctx_dma = None

    for h in range(NH):
        pc, jj = h // 2, h % 2
        for part, dst in ((0, dn), (1, kn)):
            buf = S["wt0" if part == 0 else "wt1"]
            p.wait("gpsimd", ev_wt_free[part])
            ev_w = p.dma("gpsimd", buf[:, :, 0:64],
                         dr["w_qkvT"][:, part * 1024 + h * 64: part * 1024 + (h + 1) * 64]
                         .rearrange("(k p) f -> p k f", p=128))
            bias8 = S["b_qkv8"][jj * 64:(jj + 1) * 64, (part * 8 + pc):(part * 8 + pc) + 1]
            for th in range(2):
                tsl = slice(th * 2048, (th + 1) * 2048)
                p.wait("tensor", ev_w, psa_free)
                for k in range(8):
                    for ns in range(4):
                        p.op("tensor", lambda t, k=k, ns=ns, th=th, buf=buf: t.matmul(
                            PSA[0:64, ns * 512:(ns + 1) * 512],
                            buf[:, k, 0:64],
                            xln[:, k, th * 2048 + ns * 512: th * 2048 + (ns + 1) * 512],
                            start=(k == 0), stop=(k == 7)))
                ev_mm = p.sig("tensor")
                if th == 1:
                    ev_wt_free[part] = ev_mm
                p.wait("scalar", ev_mm)
                psa_free = p.op_sig("scalar", lambda s, dst=dst, tsl=tsl, bias8=bias8: s.activation(
                    dst[0:64, tsl], PSA[0:64, 0:2048], AF.Identity, bias=bias8, scale=0.125))
                p.op("scalar", lambda s, dst=dst, tsl=tsl: s.activation(
                    S["sqt"][:], dst[0:64, tsl], AF.Square))
                ev_sq = p.sig("scalar")
                p.wait("tensor", ev_sq, psb_free[0], psb_free[1], psb_free[2], psb_free[3])
                for ns in range(4):
                    p.op("tensor", lambda t, ns=ns: t.matmul(
                        PSB[0:1, ns * 512:(ns + 1) * 512], S["neghalf_bf"][:],
                        S["sqt"][0:64, ns * 512:(ns + 1) * 512], start=True, stop=True))
                ev_n = p.sig("tensor")
                p.wait("scalar", ev_n)
                ev_ne = p.op_sig("scalar", lambda s, dst=dst, tsl=tsl: s.activation(
                    dst[64:65, tsl], PSB[0:1, 0:2048], AF.Copy))
                for b in range(4):
                    psb_free[b] = ev_ne

        ev_qp = None
        for part, src, dstp in ((0, dn, QP), (1, kn, KP)):
            for bat in range(4):
                bank = bat % 2
                p.wait("tensor", ev_pm, psb_free[bank])
                for j8 in range(8):
                    tc = bat * 8 + j8
                    p.op("tensor", lambda t, bank=bank, j8=j8, tc=tc, src=src: t.matmul(
                        PSB[:, bank * 512 + j8 * 64: bank * 512 + (j8 + 1) * 64],
                        src[:, tc * 128:(tc + 1) * 128],
                        S["pm_bf"][:], start=True, stop=True))
                ev_mm = p.sig("tensor")
                psv = PSB[:, bank * 512:(bank + 1) * 512].rearrange(
                    "p (a b) -> p a b", b=64)
                p.wait("vector", ev_mm)
                p.op("vector", lambda v, psv=psv: v.reduce_max(
                    mx[:].rearrange("p (a b) -> p a b", b=1), psv, axis=AX.X,
                    negate=True))
                p.op("vector", lambda v, psv=psv: v.scalar_tensor_tensor(
                    subt[:].rearrange("p (a b) -> p a b", b=64), psv, -LN2_8,
                    mx[:].rearrange("p (a b) -> p a b", b=1).broadcast_to([128, 8, 64]),
                    ALU.add, ALU.add))
                ev_sub = p.sig("vector")
                psb_free[bank] = ev_sub
                p.wait("scalar", ev_sub)
                ev_e = p.op_sig("scalar", lambda s: s.activation(
                    etmp[:], subt[:], AF.Exp))
                p.wait("vector", ev_e)
                ev_qp = p.op_sig("vector", lambda v, dstp=dstp, bat=bat: v.tensor_scalar_add(
                    dstp[:, bat * 512:(bat + 1) * 512], etmp[:], 1e-6))
        ev_probs = ev_qp

        p.wait("tensor", ev_probs, psb_free[3])
        for tc in range(32):
            p.op("tensor", lambda t, tc=tc, pc=pc, jj=jj: t.matmul(
                PSB[0:64, 1536:1600],
                KP[:, tc * 64:(tc + 1) * 64],
                VT[:, tc, pc, jj * 64:(jj + 1) * 64],
                start=(tc == 0), stop=(tc == 31)))
        for tc in range(32):
            p.op("tensor", lambda t, tc=tc: t.matmul(
                PSB[0:64, 1600:1601],
                KP[:, tc * 64:(tc + 1) * 64],
                ones_col[:], start=(tc == 0), stop=(tc == 31)))
        ev_kvm = p.sig("tensor")
        p.wait("scalar", ev_kvm)
        p.op("scalar", lambda s: s.activation(kv_sb[0:64, 0:64], PSB[0:64, 1536:1600], AF.Copy))
        ev_kv = p.op_sig("scalar", lambda s: s.activation(
            kv_sb[0:64, 64:65], PSB[0:64, 1600:1601], AF.Copy))
        psb_free[3] = ev_kv

        ev_t2 = None
        for g in range(8):
            p.wait("tensor", psb_free[2])
            for j in range(4):
                tc = g * 4 + j
                p.op("tensor", lambda t, j=j, tc=tc: t.matmul(
                    PSB[0:64, 1024 + j * 128: 1024 + (j + 1) * 128],
                    QP[:, tc * 64:(tc + 1) * 64],
                    ident[:], start=True, stop=True))
            ev_t = p.sig("tensor")
            p.wait("scalar", ev_t)
            ev_t2 = p.op_sig("scalar", lambda s, g=g: s.activation(
                QPF[0:64, g * 512:(g + 1) * 512], PSB[0:64, 1024:1536], AF.Copy))
            psb_free[2] = ev_t2

        for th in range(2):
            tsl = slice(th * 2048, (th + 1) * 2048)
            p.wait("tensor", ev_kv, ev_t2, psa_free)
            for ns in range(4):
                p.op("tensor", lambda t, ns=ns, th=th: t.matmul(
                    PSA[0:65, ns * 512:(ns + 1) * 512],
                    kv_sb[0:64, 0:65],
                    QPF[0:64, th * 2048 + ns * 512: th * 2048 + (ns + 1) * 512],
                    start=True, stop=True))
            ev_o = p.sig("tensor")
            p.wait("vector", ev_o)
            p.op("vector", lambda v: v.tensor_scalar_add(nrow[:], PSA[64:65, 0:2048], 1e-6))
            ev_rc = p.op_sig("vector", lambda v: v.reciprocal(S["nrow_r"][:], nrow[:]))
            p.wait("tensor", ev_rc, psb_free[0], psb_free[1], psb_free[2], psb_free[3])
            for ns in range(4):
                p.op("tensor", lambda t, ns=ns: t.matmul(
                    PSB[0:64, ns * 512:(ns + 1) * 512],
                    ones_row[0:1, 0:64],
                    S["nrow_r"][0:1, ns * 512:(ns + 1) * 512], start=True, stop=True))
            ev_rep = p.sig("tensor")
            p.wait("scalar", ev_o)
            ev_cf = p.op_sig("scalar", lambda s: s.activation(ctx_f[:], PSA[0:64, 0:2048], AF.Copy))
            p.wait("vector", ev_rep, ev_cf, ev_ctx_dma)
            ev_cb = p.op_sig("vector", lambda v: v.tensor_tensor(
                ctx_hb[:], ctx_f[:], PSB[0:64, 0:2048], ALU.mult))
            psa_free = ev_cb
            for b in range(4):
                psb_free[b] = ev_cb
            p.wait("sync", ev_cb)
            ev_ctx_dma = p.dma(
                "sync", dr["ctx_scr"][h * 64:(h + 1) * 64, tsl], ctx_hb[:])

        if dr.get("dbg_dn") is not None and h == 0:
            pb = p.barrier()
            for nm, tile in (("dbg_dn", dn[:]), ("dbg_kn", kn[:]), ("dbg_qp", QP[:]),
                             ("dbg_kp", KP[:]), ("dbg_kv", kv_sb[:]), ("dbg_qpf", QPF[:])):
                p.dma("sync", dr[nm], tile)
            p.barrier()

    p.barrier()
    CTX = S["CTX"]
    ev_ctx = p.dma("sync", CTX[:], dr["ctx_scr"].rearrange("(f p) t -> p f t", p=128))
    ytmp = S["ytmp"]
    y_view = fm_view(dr["y_scr"])
    ev_y_st = [None]

    def proj_evac(mc, th, ev_mm):
        p.wait("scalar", ev_mm, ev_y_st[0])
        ev_y = p.op_sig("scalar", lambda s, mc=mc: s.activation(
            ytmp[:], PSA[:, 0:2048], AF.Identity, bias=S["b_proj"][:, mc:mc + 1]))
        p.wait("sync", ev_y)
        ev_y_st[0] = p.dma("sync", y_view[:, mc, th * 2048:(th + 1) * 2048], ytmp[:])
        return ev_y

    p.wait("tensor", ev_ctx)
    emit_gemm(p, S, CTX, dr["w_projT"], 0, 8, proj_evac)
    return


def emit_gate_l0(p, S, xln, dr):
    PSA, PSB = S["PSA"], S["PSB"]
    scF = S["scF"]
    cmbt = S["cmbt_sb"]
    CMB = S["CMB"]
    svt = S["svt"]
    idf = S["ident_f32"]

    ev_w = p.dma("gpsimd", S["wgt"][:], dr["w_gateT"].rearrange("(k p) f -> p k f", p=128))
    psa_free = None
    for tg in range(8):
        p.wait("tensor", ev_w, psa_free)
        for k in range(8):
            p.op("tensor", lambda t, k=k, tg=tg: t.matmul(
                PSA[0:16, 0:512], S["wgt"][:, k, :],
                xln[:, k, tg * 512:(tg + 1) * 512], start=(k == 0), stop=(k == 7)))
        ev_mm = p.sig("tensor")
        p.wait("scalar", ev_mm)
        psa_free = p.op_sig("scalar", lambda s, tg=tg: s.activation(
            scF[:, tg * 512:(tg + 1) * 512], PSA[0:16, 0:512], AF.Sigmoid))
    psb_free = None
    ev_cmb = None
    for bat in range(4):
        p.wait("tensor", psa_free, psb_free)
        for j in range(8):
            tc = bat * 8 + j
            p.op("tensor", lambda t, j=j, tc=tc: t.matmul(
                PSB[:, j * 16:(j + 1) * 16], scF[:, tc * 128:(tc + 1) * 128],
                idf[0:16, 0:16], start=True, stop=True))
        ev_t = p.sig("tensor")
        p.wait("vector", ev_t)
        psb_free = p.op_sig("vector", lambda v: v.tensor_copy(
            svt[:].rearrange("p a b -> p (a b)"), PSB[:, 0:128]))
        sv16 = svt[:].rearrange("p a b -> p a b")
        sv4 = svt[:].rearrange("p a (g c) -> p a g c", c=4)
        g1 = S["g_r1"][:].rearrange("p (a g c) -> p a g c", a=8, c=1)
        g1f = S["g_r1"][:].rearrange("p (a g) -> p a g", a=8)
        h1 = S["g_r2"][:].rearrange("p (a c) -> p a c", a=8)[:, :, 0:1]
        h2 = S["g_r3"][:].rearrange("p (a c) -> p a c", a=8)[:, :, 0:1]
        t44 = S["g_t44"][:].rearrange("p (a g c) -> p a g c", a=8, c=4)
        m44 = S["g_m44"][:].rearrange("p (a g c) -> p a g c", a=8, c=4)
        gsum = S["g_gs"][:].rearrange("p (a g) -> p a g", a=8)
        gall = S["g_m4"][:].rearrange("p (a g) -> p a g", a=8)
        m16 = S["g_b16"][:].rearrange("p (a b) -> p a b", a=8)
        t16 = S["g_a16"][:].rearrange("p (a b) -> p a b", a=8)
        u16 = S["g_c16"][:].rearrange("p (a b) -> p a b", a=8)
        ws16 = S["g_ws"][:].rearrange("p (a b) -> p a b", a=8)
        V = "vector"
        p.opw(V, lambda v: v.reduce_max(g1, sv4, axis=AX.X))
        p.opw(V, lambda v: v.tensor_tensor(t44, sv4, g1.broadcast_to([128, 8, 4, 4]), ALU.is_ge))
        p.opw(V, lambda v: v.scalar_tensor_tensor(m44, t44, -1e30, sv4, ALU.mult, ALU.add))
        p.opw(V, lambda v: v.reduce_max(m44[:, :, :, 0:1], m44, axis=AX.X))
        p.opw(V, lambda v: v.tensor_tensor(gsum, g1f, m44[:, :, :, 0], ALU.add))
        p.opw(V, lambda v: v.reduce_max(h1, gsum, axis=AX.X))
        p.opw(V, lambda v: v.tensor_tensor(gall, gsum, h1.broadcast_to([128, 8, 4]), ALU.is_ge))
        p.opw(V, lambda v: v.scalar_tensor_tensor(m44[:, :, :, 0], gall, -1e30, gsum, ALU.mult, ALU.add))
        p.opw(V, lambda v: v.reduce_max(h2, m44[:, :, :, 0], axis=AX.X))
        p.opw(V, lambda v: v.tensor_tensor(gall, gsum, h2.broadcast_to([128, 8, 4]), ALU.is_lt))
        p.opw(V, lambda v: v.scalar_tensor_tensor(
            m16.rearrange("p a (g c) -> p a g c", c=4),
            gall.rearrange("p a (g c) -> p a g c", c=1).broadcast_to([128, 8, 4, 4]),
            -1e30, sv4, ALU.mult, ALU.add))
        cur = m16
        msk = ws16
        for it in range(3):
            nxt = u16 if it % 2 == 0 else t16
            p.opw(V, lambda v, cur=cur: v.reduce_max(h1, cur, axis=AX.X))
            p.opw(V, lambda v, cur=cur: v.tensor_tensor(
                msk, cur, h1.broadcast_to([128, 8, 16]), ALU.is_ge))
            p.opw(V, lambda v, cur=cur, nxt=nxt: v.scalar_tensor_tensor(
                nxt, msk, -1e30, cur, ALU.mult, ALU.add))
            cur = nxt
        p.opw(V, lambda v, cur=cur: v.reduce_max(h1, cur, axis=AX.X))
        p.opw(V, lambda v: v.tensor_tensor(t16, m16, h1.broadcast_to([128, 8, 16]), ALU.is_ge))
        p.opw(V, lambda v: v.tensor_tensor(ws16, sv16, t16, ALU.mult))
        p.opw(V, lambda v: v.reduce_sum(h1, ws16, axis=AX.X))
        p.opw(V, lambda v: v.tensor_scalar_add(h2, h1, 1e-6))
        p.opw(V, lambda v: v.reciprocal(h1, h2))
        ev_cmb = p.opw(V, lambda v, bat=bat: v.tensor_tensor(
            CMB[:, bat * 128:(bat + 1) * 128].rearrange("p (a b) -> p a b", a=8),
            ws16, h1.broadcast_to([128, 8, 16]), ALU.mult))
    psb_free2 = None
    ev_ce = None
    for bat in range(8):
        p.wait("tensor", ev_cmb, psb_free2)
        for j in range(4):
            tc = bat * 4 + j
            p.op("tensor", lambda t, j=j, tc=tc: t.matmul(
                PSB[0:16, j * 128:(j + 1) * 128], CMB[:, tc * 16:(tc + 1) * 16],
                idf[:], start=True, stop=True))
        ev_t = p.sig("tensor")
        p.wait("vector", ev_t)
        psb_free2 = p.op_sig("vector", lambda v, bat=bat: v.tensor_copy(
            cmbt[:, bat * 512:(bat + 1) * 512], PSB[0:16, 0:512]))
        ev_ce = psb_free2
    p.wait("sync", ev_ce)
    ev = p.dma("sync", dr["cmbt"], cmbt[:])
    return ev


def emit_l0_ffn(p, S, xln, dr, n_experts=16, do_shared=True):
    PSA, PSB = S["PSA"], S["PSB"]
    E, F = 16, 512
    routed = S["routed"]
    gu = S["gu"]
    s_sb = S["s_sb"]
    crep_sb = S["crep_sb"]
    crow = S["crow"]
    w1t, w3t, w2t = S["w1t"], S["w3t"], S["w2t"]
    ones_row = S["ones_row_f32"]

    ev_gate = emit_gate_l0(p, S, xln, dr)
    y_view = fm_view(dr["y_scr"])

    ev_rt_st = None
    ev_gu_done = None
    last_w_use = None
    for th in range(2):
        tsl0 = th * 2048
        p.wait("vector", ev_rt_st, ev_gate)
        ev_z = p.op_sig("vector", lambda v: v.memset(
            routed[:].rearrange("p a b -> p (a b)"), 0.0))
        ev_acc = ev_z
        for e in range(n_experts):
            p.wait("gpsimd", last_w_use)
            ev_w1 = p.dma("gpsimd", w1t[:], dr["w1eT"][e].rearrange("(k p) f -> p k f", p=128))
            ev_w3 = p.dma("gpsimd", w3t[:], dr["w3eT"][e].rearrange("(k p) f -> p k f", p=128))
            ev_w2 = p.dma("gpsimd", w2t[:], dr["w2eT"][e].rearrange("(k p) f -> p k f", p=128))
            p.wait("sync", ev_gate, S.get("_crm"))
            ev_cr = p.dma("sync", crow[:], dr["cmbt"][e:e + 1, tsl0:tsl0 + 2048])
            p.wait("tensor", ev_cr, S.get("_psb_free"))
            for ns in range(4):
                p.op("tensor", lambda t, ns=ns: t.matmul(
                    PSB[:, ns * 512:(ns + 1) * 512], ones_row[:],
                    crow[0:1, ns * 512:(ns + 1) * 512], start=True, stop=True))
            ev_crm = p.sig("tensor")
            S["_crm"] = ev_crm
            p.wait("scalar", ev_crm)
            ev_cre = p.op_sig("scalar", lambda s: s.activation(crep_sb[:], PSB[:, 0:2048], AF.Copy))
            S["_psb_free"] = ev_cre
            psa_free = S.get("_psa_free")
            for fc in range(4):
                p.wait("tensor", ev_w1, psa_free)
                for k in range(8):
                    for ns in range(4):
                        p.op("tensor", lambda t, k=k, ns=ns, fc=fc, tsl0=tsl0: t.matmul(
                            PSA[:, ns * 512:(ns + 1) * 512], w1t[:, k, fc * 128:(fc + 1) * 128],
                            xln[:, k, tsl0 + ns * 512: tsl0 + (ns + 1) * 512],
                            start=(k == 0), stop=(k == 7)))
                ev_h1 = p.sig("tensor")
                p.wait("scalar", ev_h1, ev_gu_done)
                ev_si = p.op_sig("scalar", lambda s: s.activation(s_sb[:], PSA[:, 0:2048], AF.Silu))
                psa_free = ev_si
                p.wait("tensor", ev_w3, ev_cre, ev_gu_done)
                for k in range(8):
                    for ns in range(4):
                        p.op("tensor", lambda t, k=k, ns=ns, fc=fc, tsl0=tsl0: t.matmul(
                            PSB[:, ns * 512:(ns + 1) * 512], w3t[:, k, fc * 128:(fc + 1) * 128],
                            xln[:, k, tsl0 + ns * 512: tsl0 + (ns + 1) * 512],
                            start=(k == 0), stop=(k == 7)))
                ev_h3 = p.sig("tensor")
                p.wait("vector", ev_h3, ev_si)
                if dr.get("dbg_h3") is not None and e == 0 and th == 0 and fc == 0:
                    p.op("vector", lambda v: v.tensor_copy(routed[:, 7, :], PSB[:, 0:2048]))
                    ev_d = p.sig("vector")
                    p.wait("sync", ev_d)
                    p.dma("sync", dr["dbg_h3"], routed[:, 7, :])
                    p.wait("sync", p.sig("scalar") if False else ev_d)
                    p.dma("sync", dr["dbg_si"], s_sb[:])
                p.op("vector", lambda v: v.tensor_mul(s_sb[:], s_sb[:], PSB[:, 0:2048]))
                ev_gu_done = p.op_sig("vector", lambda v, fc=fc: v.tensor_mul(
                    gu[:, fc, :], s_sb[:], crep_sb[:]))
                S["_psb_free"] = ev_gu_done
                if dr.get("dbg_gu") is not None and e == 0 and th == 0 and fc == 0:
                    p.wait("sync", ev_gu_done)
                    p.dma("sync", dr["dbg_gu"], gu[:, 0, :])
                    p.dma("sync", dr["dbg_crep"], crep_sb[:])
                    p.dma("sync", dr["dbg_w3t"], w3t[:].rearrange("p a b -> p (a b)"))
                    p.dma("sync", dr["dbg_xln"], xln[:, 0, 0:2048])
            for hc in range(8):
                p.wait("tensor", ev_w2, psa_free, ev_gu_done)
                for k in range(4):
                    for ns in range(4):
                        p.op("tensor", lambda t, k=k, ns=ns, hc=hc: t.matmul(
                            PSA[:, ns * 512:(ns + 1) * 512], w2t[:, k, hc * 128:(hc + 1) * 128],
                            gu[:, k, ns * 512:(ns + 1) * 512], start=(k == 0), stop=(k == 3)))
                ev_y = p.sig("tensor")
                if hc == 7:
                    last_w_use = ev_y
                p.wait("vector", ev_y, ev_acc)
                ev_acc = p.op_sig("vector", lambda v, hc=hc: v.tensor_add(
                    routed[:, hc, :], routed[:, hc, :], PSA[:, 0:2048]))
                psa_free = ev_acc
            S["_psa_free"] = psa_free
        psa_free = S.get("_psa_free")
        psb_free = S.get("_psb_free")
        for half in (range(2) if do_shared else []):
            for fc in range(4):
                mc = half * 4 + fc
                p.wait("gpsimd", psa_free)
                ev_a = p.dma("gpsimd", S["wt0"][:],
                             dr["w1sT"][:, mc * 128:(mc + 1) * 128].rearrange("(k p) f -> p k f", p=128))
                p.wait("gpsimd", psb_free)
                ev_b = p.dma("gpsimd", S["wt1"][:],
                             dr["w3sT"][:, mc * 128:(mc + 1) * 128].rearrange("(k p) f -> p k f", p=128))
                p.wait("tensor", ev_a, psa_free)
                for k in range(8):
                    for ns in range(4):
                        p.op("tensor", lambda t, k=k, ns=ns, tsl0=tsl0: t.matmul(
                            PSA[:, ns * 512:(ns + 1) * 512], S["wt0"][:, k, :],
                            xln[:, k, tsl0 + ns * 512: tsl0 + (ns + 1) * 512],
                            start=(k == 0), stop=(k == 7)))
                ev_h1 = p.sig("tensor")
                p.wait("scalar", ev_h1, psb_free)
                ev_si = p.op_sig("scalar", lambda s: s.activation(s_sb[:], PSA[:, 0:2048], AF.Silu))
                psa_free = ev_si
                p.wait("tensor", ev_b, psb_free)
                for k in range(8):
                    for ns in range(4):
                        p.op("tensor", lambda t, k=k, ns=ns, tsl0=tsl0: t.matmul(
                            PSB[:, ns * 512:(ns + 1) * 512], S["wt1"][:, k, :],
                            xln[:, k, tsl0 + ns * 512: tsl0 + (ns + 1) * 512],
                            start=(k == 0), stop=(k == 7)))
                ev_h3 = p.sig("tensor")
                p.wait("vector", ev_h3, ev_si)
                psb_free = p.op_sig("vector", lambda v, fc=fc: v.tensor_mul(
                    gu[:, fc, :], s_sb[:], PSB[:, 0:2048]))
                if dr.get("dbg_h3") is not None and half == 0 and fc == 0 and th == 0:
                    p.wait("sync", psb_free)
                    p.dma("sync", dr["dbg_si"], s_sb[:])
                    p.dma("sync", dr["dbg_gu"], gu[:, 0, :])
            ev_gus = psb_free
            for hc in range(8):
                p.wait("gpsimd", psa_free)
                ev_w2s = p.dma("gpsimd", S["w2st_t"][:],
                               dr["w2sT"][:, hc * 128:(hc + 1) * 128]
                               .rearrange("(k p) f -> p k f", p=128))
                p.wait("tensor", ev_w2s, psa_free, ev_gus)
                for k in range(4):
                    kk = half * 4 + k
                    for ns in range(4):
                        p.op("tensor", lambda t, k=k, kk=kk, ns=ns: t.matmul(
                            PSA[:, ns * 512:(ns + 1) * 512], S["w2st_t"][:, kk, :],
                            gu[:, k, ns * 512:(ns + 1) * 512], start=(k == 0), stop=(k == 3)))
                ev_y = p.sig("tensor")
                p.wait("vector", ev_y, ev_acc)
                ev_acc = p.op_sig("vector", lambda v, hc=hc: v.tensor_add(
                    routed[:, hc, :], routed[:, hc, :], PSA[:, 0:2048]))
                psa_free = ev_acc
                if dr.get("dbg_h3") is not None and half == 0 and hc == 1 and th == 0:
                    p.wait("sync", ev_acc)
                    p.dma("sync", dr["dbg_h3"], routed[:, 1, :])
                    p.dma("sync", dr["dbg_si"], routed[:, 0, :])
        S["_psa_free"] = psa_free
        S["_psb_free"] = psb_free
        p.wait("sync", ev_acc)
        for hc in range(8):
            ev_rt_st = p.dma("sync", y_view[:, hc, tsl0:tsl0 + 2048], routed[:, hc, :])
    p.wait("vector", ev_rt_st)
    return


def alloc_l0ffn_sbuf(nc):
    S = {}
    def sb(name, shape, dtype):
        S[name] = nc.alloc_sbuf_tensor(name, shape, dtype).ap()
        return S[name]
    sb("XLN", [128, 8, T], BF16)
    pool = nc.alloc_sbuf_tensor("POOL", [128, 33280], BF16)
    pf = pool.bitcast(F32).ap()
    S["routed"] = pf[:, 0:16384].rearrange("p (a b) -> p a b", a=8)
    S["scF"] = pf[0:16, 0:4096]
    S["cmbt_sb"] = pf[0:16, 4096:8192]
    S["CMB"] = pf[:, 8192:8704]
    S["ln_xc"] = pf[:, 0:4096].rearrange("p (a b) -> p a b", a=8)
    S["ln_rc"] = pf[:, 4096:8192].rearrange("p (a b) -> p a b", a=8)
    S["ln_xb"] = pool.ap()[:, 16384:20480].rearrange("p (a b) -> p a b", a=8)
    S["ln_sq"] = pool.ap()[:, 20480:24576].rearrange("p (a b) -> p a b", a=8)
    S["ln_t"] = pf[:, 8192:12288].rearrange("p (a b) -> p a b", a=8)
    S["ln_rows"] = pf[0:1, 12288:15360]
    S["ln_reps"] = pf[:, 15360:16384]
    sb("w1t", [128, 8, 512], BF16)
    sb("w3t", [128, 8, 512], BF16)
    sb("w2t", [128, 4, 1024], BF16)
    sb("w2st_t", [128, 8, 128], BF16)
    sb("wt0", [128, 8, 128], BF16)
    sb("wt1", [128, 8, 128], BF16)
    sb("wgt", [128, 8, 16], BF16)
    sb("gu", [128, 4, 2048], BF16)
    sb("s_sb", [128, 2048], F32)
    sb("crep_sb", [128, 2048], BF16)
    sb("crow", [1, 2048], F32)
    sb("svt", [128, 8, 16], F32)
    sb("g_r1", [128, 32], F32); sb("g_r2", [128, 8], F32); sb("g_r3", [128, 8], F32)
    sb("g_m4", [128, 32], F32); sb("g_gs", [128, 32], F32)
    sb("g_t44", [128, 128], F32); sb("g_m44", [128, 128], F32)
    sb("g_a16", [128, 128], F32); sb("g_b16", [128, 128], F32); sb("g_c16", [128, 128], F32)
    sb("g_ws", [128, 128], F32)
    sb("ones_col_bf", [128, 1], BF16)
    sb("ones_row_f32", [1, 128], F32)
    sb("ident_f32", [128, 128], F32)
    S["PSA"] = nc.alloc_psum_tensor("PSA", [128, 2048], F32).ap()
    S["PSB"] = nc.alloc_psum_tensor("PSB", [128, 2048], F32).ap()
    return S


def build_l0ffn_block(debug=False, n_experts=16, do_shared=True):
    p = Prog()
    nc = p.nc
    dr = {}
    dr["x"] = nc.dram_tensor("x", [D, T], F32, kind="ExternalInput").ap()
    dr["w_gateT"] = nc.dram_tensor("w_gateT", [1024, 16], F32, kind="ExternalInput").ap()
    dr["w1eT"] = nc.dram_tensor("w1eT", [16, 1024, 512], F32, kind="ExternalInput").ap()
    dr["w3eT"] = nc.dram_tensor("w3eT", [16, 1024, 512], F32, kind="ExternalInput").ap()
    dr["w2eT"] = nc.dram_tensor("w2eT", [16, 512, 1024], F32, kind="ExternalInput").ap()
    dr["w1sT"] = nc.dram_tensor("w1sT", [1024, 1024], F32, kind="ExternalInput").ap()
    dr["w3sT"] = nc.dram_tensor("w3sT", [1024, 1024], F32, kind="ExternalInput").ap()
    dr["w2sT"] = nc.dram_tensor("w2sT", [1024, 1024], F32, kind="ExternalInput").ap()
    dr["c_ones_col"] = nc.dram_tensor("c_ones_col", [128, 1], F32, kind="ExternalInput").ap()
    dr["c_ones_row"] = nc.dram_tensor("c_ones_row", [1, 128], F32, kind="ExternalInput").ap()
    dr["c_ident_f32"] = nc.dram_tensor("c_ident_f32", [128, 128], F32, kind="ExternalInput").ap()
    kind_dbg = "ExternalOutput" if debug else "Internal"
    dr["cmbt"] = nc.dram_tensor("cmbt", [16, T], F32, kind=kind_dbg).ap()
    dr["y_scr"] = nc.dram_tensor("y_scr", [D, T], F32, kind=kind_dbg).ap()
    dr["xo"] = nc.dram_tensor("xo", [D, T], F32, kind="ExternalOutput").ap()
    if debug == "deep":
        dr["dbg_h3"] = nc.dram_tensor("dbg_h3", [128, 2048], F32, kind="ExternalOutput").ap()
        dr["dbg_si"] = nc.dram_tensor("dbg_si", [128, 2048], F32, kind="ExternalOutput").ap()
        dr["dbg_gu"] = nc.dram_tensor("dbg_gu", [128, 2048], mybir.dt.bfloat16, kind="ExternalOutput").ap()
        dr["dbg_crep"] = nc.dram_tensor("dbg_crep", [128, 2048], mybir.dt.bfloat16, kind="ExternalOutput").ap()
        dr["dbg_w3t"] = nc.dram_tensor("dbg_w3t", [128, 4096], mybir.dt.bfloat16, kind="ExternalOutput").ap()
        dr["dbg_xln"] = nc.dram_tensor("dbg_xln", [128, 2048], mybir.dt.bfloat16, kind="ExternalOutput").ap()

    S = alloc_l0ffn_sbuf(nc)
    evs = [
        p.dma("gpsimd", S["ones_col_bf"][:], dr["c_ones_col"]),
        p.dma("sync", S["ones_row_f32"][:], dr["c_ones_row"]),
        p.dma("sync", S["ident_f32"][:], dr["c_ident_f32"]),
    ]
    p.wait("tensor", *evs)
    p.wait("vector", *evs)
    p.wait("scalar", *evs)
    emit_ln(p, S, dr["x"], dst_xln=S["XLN"])
    p.barrier()
    emit_l0_ffn(p, S, S["XLN"], dr, n_experts=n_experts, do_shared=do_shared)
    p.barrier()
    emit_ln(p, S, dr["y_scr"], resid_dram=dr["x"], out_dram=dr["xo"])
    p.barrier()
    return p.build()


def emit_l1_attn(p, S, xln, dr):
    PSA, PSB = S["PSA"], S["PSB"]
    pool = S["_pool"]
    pf = pool.bitcast(F32).ap()
    pb = pool.ap()
    q_sb = pb[:, 0:32768].rearrange("p (a b) -> p a b", a=8)
    CTX = S["XLN"]
    ones_row = S["ones_row_f32"]
    ident = S["ident_bf"]
    ktmp = S["l1_ktmp"]
    qtmp8 = S["l1_q8"]
    ytmp = S["ytmp"]
    glo = S["l1_glo"]
    gt = S["l1_gt"]
    cbias = S["l1_cbias"]
    mk_v = dr["mk_scr"]
    mv_v = dr["mv_scr"]
    mq8_v = dr["mq8_scr"]

    st_ev = [None]

    def mk_evac(which, bcol, dst):
        def f(mc, th, ev_mm):
            p.wait("scalar", ev_mm, st_ev[0])
            ev = p.op_sig("scalar", lambda s, mc=mc: s.activation(
                ktmp[:], PSA[:, 0:2048], AF.Identity, bias=bcol[:, mc:mc + 1]))
            p.wait("sync", ev)
            st_ev[0] = p.dma("sync", dst[mc * 128:(mc + 1) * 128, th * 2048:(th + 1) * 2048], ktmp[:])
            return ev
        return f

    emit_gemm(p, S, xln, dr["l1_wk"], 0, 8, mk_evac("k", S["l1_bk"], mk_v))
    p.barrier()
    emit_gemm(p, S, xln, dr["l1_wv"], 0, 8, mk_evac("v", S["l1_bv"], mv_v))
    p.barrier()

    ev_bq8 = p.op_sig("vector", lambda v: v.tensor_scalar_mul(S["l1_bq8"][:], S["l1_bq"][:], 0.125))

    def q_evac(mc, th, ev_mm):
        p.wait("scalar", ev_mm, st_ev[0], ev_bq8)
        ev1 = p.op_sig("scalar", lambda s, mc=mc, th=th: s.activation(
            q_sb[:, mc, th * 2048:(th + 1) * 2048], PSA[:, 0:2048], AF.Identity,
            bias=S["l1_bq"][:, mc:mc + 1]))
        ev2 = p.op_sig("scalar", lambda s, mc=mc: s.activation(
            qtmp8[:], PSA[:, 0:2048], AF.Identity, bias=S["l1_bq8"][:, mc:mc + 1], scale=0.125))
        p.wait("sync", ev2)
        st_ev[0] = p.dma("sync", mq8_v[mc * 128:(mc + 1) * 128, th * 2048:(th + 1) * 2048], qtmp8[:])
        p.wait("vector", ev2)
        p.op("vector", lambda v, th=th: v.reduce_sum(gt[:, th:th + 1], PSA[:, 0:2048], axis=AX.X))
        if th == 1:
            p.op("vector", lambda v: v.tensor_add(gt[:, 0:1], gt[:, 0:1], gt[:, 1:2]))
            p.op("vector", lambda v, mc=mc: v.scalar_tensor_tensor(
                glo[:, mc:mc + 1], gt[:, 0:1], 1.0 / T, S["l1_bq"][:, mc:mc + 1],
                ALU.mult, ALU.add))
        ev3 = p.sig("vector")
        return ev3

    p.barrier()
    emit_gemm(p, S, xln, dr["l1_wq"], 0, 8, q_evac)
    p.barrier()

    ev_bs = p.op_sig("vector", lambda v: v.tensor_add(S["l1_bsum"][:], S["l1_bfq"][:], S["l1_bfg"][:]))
    ev_gbf = p.opw("vector", lambda v: v.tensor_copy(S["l1_glob"][:], glo[:]))
    p.wait("tensor", ev_gbf)
    wt0 = S["wt0"]
    last_use = None
    for mc in range(8):
        p.wait("gpsimd", last_use)
        ev_w = p.dma("gpsimd", wt0[:],
                     dr["l1_wfg"][:, mc * 128:(mc + 1) * 128].rearrange("(k p) f -> p k f", p=128))
        p.wait("tensor", ev_w, S.get("_l1g"))
        for k in range(8):
            p.op("tensor", lambda t, k=k: t.matmul(
                PSA[0:128, 0:1], wt0[:, k, :], S["l1_glob"][:, k:k + 1], start=(k == 0), stop=(k == 7)))
        ev_mm = p.sig("tensor")
        last_use = ev_mm
        p.wait("scalar", ev_mm, ev_bs)
        S["_l1g"] = p.op_sig("scalar", lambda s, mc=mc: s.activation(
            cbias[:, mc:mc + 1], PSA[0:128, 0:1], AF.Identity, bias=S["l1_bsum"][:, mc:mc + 1]))
    p.barrier()

    ev_uv = p.dma("gpsimd", S["l1_uvt"][:],
                  dr["l1_uvdiag"].rearrange("(k p) f -> p k f", p=128))
    pz_sb = S["l1_pz"]
    ctmp = S["l1_ctmp"]
    for th in range(2):
        psb_ev = None
        ev_ct = None
        last_w = None
        for mc in range(8):
            p.wait("gpsimd", last_w)
            ev_w = p.dma("gpsimd", wt0[:],
                         dr["l1_wfq"][:, mc * 128:(mc + 1) * 128].rearrange("(k p) f -> p k f", p=128))
            p.wait("tensor", ev_w, S.get("_l1c"))
            for k in range(8):
                for ns in range(4):
                    p.op("tensor", lambda t, k=k, ns=ns, th=th: t.matmul(
                        PSA[:, ns * 512:(ns + 1) * 512], wt0[:, k, :],
                        q_sb[:, k, th * 2048 + ns * 512: th * 2048 + (ns + 1) * 512],
                        start=(k == 0), stop=(k == 7)))
            ev_mm = p.sig("tensor")
            last_w = ev_mm
            p.wait("scalar", ev_mm, ev_ct)
            ev_c = p.op_sig("scalar", lambda s, mc=mc: s.activation(
                ctmp[:], PSA[:, 0:2048], AF.Tanh, bias=cbias[:, mc:mc + 1]))
            S["_l1c"] = ev_c
            p.wait("tensor", ev_c, ev_uv, S.get("_l1pz"))
            for ns in range(4):
                p.op("tensor", lambda t, mc=mc, ns=ns: t.matmul(
                    PSB[0:64, ns * 512:(ns + 1) * 512], S["l1_uvt"][:, mc, :],
                    ctmp[:, ns * 512:(ns + 1) * 512],
                    start=(mc == 0), stop=(mc == 7)))
            ev_ct = p.sig("tensor")
        p.wait("scalar", ev_ct)
        ev_pz = p.op_sig("scalar", lambda s, th=th: s.activation(
            pz_sb[:, th * 2048:(th + 1) * 2048], PSB[0:64, 0:2048], AF.Copy))
        S["_l1pz"] = ev_pz
    p.barrier()

    SP = pf[0:16, 0:4096]
    ZR = pf[0:16, 4096:8192]
    SZ = pf[0:16, 8192:12288]
    T1 = pf[0:16, 12288:16384]
    abc_a = pb[0:16, 16384:20480]
    abc_b = pb[0:16, 20480:24576]
    abc_c = pb[0:16, 24576:28672]
    SCL = "scalar"; V = "vector"
    ev_zr = p.dma("sync", ZR, pz_sb[32:48, :])
    p.opw(SCL, lambda s: s.activation(SP, pz_sb[0:16, :], AF.Sigmoid))
    p.wait(SCL, ev_zr)
    p.opw(SCL, lambda s: s.activation(SZ, ZR, AF.Sigmoid))
    ev_t1 = p.opw(SCL, lambda s: s.activation(T1, SZ, AF.Square))
    p.wait(V, ev_t1)
    p.opw(V, lambda v: v.reciprocal(SZ, T1))
    p.opw(V, lambda v: v.tensor_scalar_add(T1, SP, -0.5))
    p.opw(V, lambda v: v.tensor_scalar_mul(abc_a, SZ, -1.1920928955078125e-07))
    p.opw(V, lambda v: v.tensor_mul(SP, T1, SZ))
    p.opw(V, lambda v: v.tensor_scalar_mul(abc_b, SP, 0.0009765625))
    p.opw(V, lambda v: v.tensor_mul(ZR, T1, SP))
    ev_abc = p.opw(V, lambda v: v.tensor_scalar_mul(abc_c, ZR, -2.0))
    ev_iota = p.dma("gpsimd", S["l1_iota"][:], dr["c_iota"])
    p.barrier()

    kaug = S["l1_kaug"]; qaug = S["l1_qaug"]; mvh = S["l1_mvh"]; vaug = S["l1_vaug"]
    prb = S["l1_prb"]; ctx_f = S["l1_ctxf"]; nr_r = S["l1_nr"]
    ev_tail = {"kaug": None, "qaug": None, "vaug": None, "mvh": None}
    psb_free = None
    psa_free = None
    ev_ctx_w = None
    for h in range(NH):
        pcc, jj = h // 2, h % 2
        p.wait("sync", ev_tail["kaug"])
        ev_k = p.dma("sync", kaug[0:64, :], mk_v[h * 64:(h + 1) * 64, :])
        ev_k2 = p.dma("sync", kaug[64:67, :], S["l1_iota"][:])
        p.wait("sync", ev_tail["qaug"])
        ev_q = p.dma("sync", qaug[0:64, :], mq8_v[h * 64:(h + 1) * 64, :])
        p.dma("sync", qaug[64:65, :], abc_a[h:h + 1, :])
        p.dma("sync", qaug[65:66, :], abc_b[h:h + 1, :])
        ev_q2 = p.dma("sync", qaug[66:67, :], abc_c[h:h + 1, :])
        p.wait("sync", ev_tail["mvh"])
        ev_v = p.dma("sync", mvh[:], mv_v[h * 64:(h + 1) * 64, :])
        if h == 0:
            p.wait("sync", ev_iota)
            p.wait("gpsimd", ev_abc)
        p.wait("vector", ev_tail["vaug"])
        ev_vo = p.op_sig("vector", lambda v: v.memset(
            vaug[:].rearrange("p a b -> p (a b)"), 1.0))
        for g in range(4):
            p.wait("tensor", ev_v, ev_vo, psb_free)
            for j in range(8):
                tc = g * 8 + j
                p.op("tensor", lambda t, g=g, j=j, tc=tc: t.matmul(
                    PSB[:, (g % 2) * 1024 + j * 64:(g % 2) * 1024 + (j + 1) * 64],
                    mvh[:, tc * 128:(tc + 1) * 128], ident[0:64, 0:64],
                    start=True, stop=True))
            ev_t = p.sig("tensor")
            p.wait("scalar", ev_t)
            ev_ve = p.op_sig("scalar", lambda s, g=g: s.activation(
                vaug[:, g * 8:(g + 1) * 8, 0:64], PSB[:, (g % 2) * 1024:(g % 2) * 1024 + 512],
                AF.Copy))
            psb_free = ev_ve
        ev_tail["mvh"] = ev_t
        ev_vaug_ready = psb_free

        for qg in range(8):
            qsl = slice(qg * 512, (qg + 1) * 512)
            ev_pv = None
            ev_prb = None
            for kb in range(8):
                p.wait("tensor", ev_k, ev_k2, ev_q, ev_q2, psa_free)
                for j in range(4):
                    kc = kb * 4 + j
                    p.op("tensor", lambda t, j=j, kc=kc, qsl=qsl: t.matmul(
                        PSA[:, j * 512:(j + 1) * 512],
                        kaug[:, kc * 128:(kc + 1) * 128], qaug[:, qsl],
                        start=True, stop=True))
                ev_s = p.sig("tensor")
                if kb == 7:
                    ev_tail["kaug"] = ev_s
                    ev_tail["qaug"] = ev_s
                p.wait("scalar", ev_s, ev_pv)
                ev_prb = p.op_sig("scalar", lambda s: s.activation(
                    prb[:], PSA[:, 0:2048], AF.Exp))
                psa_free = ev_prb
                p.wait("tensor", ev_prb, ev_vaug_ready)
                for j in range(4):
                    kc = kb * 4 + j
                    p.op("tensor", lambda t, j=j, kc=kc, kb=kb: t.matmul(
                        PSB[0:65, 0:512], vaug[:, kc, :],
                        prb[:, j * 512:(j + 1) * 512],
                        start=(kb == 0 and j == 0), stop=(kb == 7 and j == 3)))
                ev_pv = p.sig("tensor")
                if kb == 7:
                    ev_tail["vaug"] = ev_pv
            p.wait("vector", ev_pv)
            ev_nr = p.opw("vector", lambda v: v.reciprocal(nr_r[:], PSB[64:65, 0:512]))
            p.wait("tensor", ev_nr, psb_free)
            p.op("tensor", lambda t: t.matmul(
                PSB[0:64, 512:1024], ones_row[0:1, 0:64], nr_r[:], start=True, stop=True))
            ev_rep = p.sig("tensor")
            p.wait("scalar", ev_pv, ev_ctx_w)
            ev_cf = p.op_sig("scalar", lambda s: s.activation(ctx_f[:], PSB[0:64, 0:512], AF.Copy))
            p.wait("vector", ev_rep, ev_cf)
            ev_ctx_w = p.op_sig("vector", lambda v, pcc=pcc, jj=jj, qsl=qsl: v.tensor_mul(
                CTX[jj * 64:(jj + 1) * 64, pcc, qsl], ctx_f[:], PSB[0:64, 512:1024]))
            psb_free = ev_ctx_w
    p.barrier()

    y_view = fm_view(dr["y_scr"])
    ev_y_st = [None]

    def o_evac(mc, th, ev_mm):
        p.wait("scalar", ev_mm, ev_y_st[0])
        ev_y = p.op_sig("scalar", lambda s, mc=mc: s.activation(
            ytmp[:], PSA[:, 0:2048], AF.Identity, bias=S["l1_bo"][:, mc:mc + 1]))
        p.wait("sync", ev_y)
        ev_y_st[0] = p.dma("sync", y_view[:, mc, th * 2048:(th + 1) * 2048], ytmp[:])
        return ev_y

    emit_gemm(p, S, CTX, dr["l1_wo"], 0, 8, o_evac)
    return


def alloc_l1attn_sbuf(nc):
    S = {}
    def sb(name, shape, dtype):
        S[name] = nc.alloc_sbuf_tensor(name, shape, dtype).ap()
        return S[name]
    sb("XLN", [128, 8, T], BF16)
    pool = nc.alloc_sbuf_tensor("POOL", [128, 33280], BF16)
    S["_pool"] = pool
    pf = pool.bitcast(F32).ap()
    pb = pool.ap()
    S["ln_xc"] = pf[:, 0:4096].rearrange("p (a b) -> p a b", a=8)
    S["ln_rc"] = pf[:, 4096:8192].rearrange("p (a b) -> p a b", a=8)
    S["ln_xb"] = pb[:, 16384:20480].rearrange("p (a b) -> p a b", a=8)
    S["ln_sq"] = pb[:, 20480:24576].rearrange("p (a b) -> p a b", a=8)
    S["ln_t"] = pf[:, 8192:12288].rearrange("p (a b) -> p a b", a=8)
    S["ln_rows"] = pf[0:1, 12288:15360]
    S["ln_reps"] = pf[:, 15360:16384]
    sb("wt0", [128, 8, 128], BF16)
    sb("wt1", [128, 8, 128], BF16)
    sb("ytmp", [128, 2048], F32)
    S["nrow_r"] = S["ytmp"][0:1, 0:2048]
    sb("l1_ktmp", [128, 2048], BF16)
    S["l1_q8"] = S["l1_ktmp"]
    S["l1_ctmp"] = S["l1_ktmp"]
    sb("l1_glo", [128, 8], F32)
    sb("l1_glob", [128, 8], BF16)
    sb("l1_gt", [128, 2], F32)
    sb("l1_cbias", [128, 8], F32)
    sb("l1_bsum", [128, 8], F32)
    sb("l1_bq8", [128, 8], F32)
    sb("l1_uvt", [128, 8, 64], BF16)
    sb("l1_pz", [64, T], F32)
    sb("l1_iota", [3, T], BF16)
    sb("l1_kaug", [67, T], BF16)
    sb("l1_qaug", [67, T], BF16)
    sb("l1_mvh", [64, T], BF16)
    sb("l1_vaug", [128, 32, 65], BF16)
    S["l1_prb"] = S["l1_ktmp"]
    sb("l1_ctxf", [64, 512], F32)
    sb("l1_nr", [1, 512], F32)
    for nm in ["bq", "bk", "bv", "bo", "bfq", "bfg"]:
        S["l1_" + nm] = nc.alloc_sbuf_tensor("sb_l1_" + nm, [128, 8], F32).ap()
    sb("ones_col_bf", [128, 1], BF16)
    sb("ones_row_f32", [1, 128], F32)
    sb("ident_bf", [128, 128], BF16)
    S["PSA"] = nc.alloc_psum_tensor("PSA", [128, 2048], F32).ap()
    S["PSB"] = nc.alloc_psum_tensor("PSB", [128, 2048], F32).ap()
    return S


def build_l1attn_block(debug=False):
    p = Prog()
    nc = p.nc
    dr = {}
    dr["x"] = nc.dram_tensor("x", [D, T], F32, kind="ExternalInput").ap()
    for nm in ["wq", "wk", "wv", "wo", "wfq", "wfg"]:
        dr["l1_" + nm] = nc.dram_tensor("l1_" + nm, [1024, 1024], F32, kind="ExternalInput").ap()
    for nm in ["bq", "bk", "bv", "bo", "bfq", "bfg"]:
        dr["l1_" + nm] = nc.dram_tensor("l1_" + nm, [128, 8], F32, kind="ExternalInput").ap()
    dr["l1_uvdiag"] = nc.dram_tensor("l1_uvdiag", [1024, 64], F32, kind="ExternalInput").ap()
    dr["c_iota"] = nc.dram_tensor("c_iota", [3, T], F32, kind="ExternalInput").ap()
    dr["c_ones_col"] = nc.dram_tensor("c_ones_col", [128, 1], F32, kind="ExternalInput").ap()
    dr["c_ones_row"] = nc.dram_tensor("c_ones_row", [1, 128], F32, kind="ExternalInput").ap()
    dr["c_ident"] = nc.dram_tensor("c_ident", [128, 128], F32, kind="ExternalInput").ap()
    kind_dbg = "ExternalOutput" if debug else "Internal"
    dr["mk_scr"] = nc.dram_tensor("mk_scr", [D, T], BF16, kind=kind_dbg).ap()
    dr["mv_scr"] = nc.dram_tensor("mv_scr", [D, T], BF16, kind=kind_dbg).ap()
    dr["mq8_scr"] = nc.dram_tensor("mq8_scr", [D, T], BF16, kind=kind_dbg).ap()
    dr["y_scr"] = nc.dram_tensor("y_scr", [D, T], F32, kind=kind_dbg).ap()
    dr["xo"] = nc.dram_tensor("xo", [D, T], F32, kind="ExternalOutput").ap()

    S = alloc_l1attn_sbuf(nc)
    evs = [
        p.dma("gpsimd", S["ones_col_bf"][:], dr["c_ones_col"]),
        p.dma("sync", S["ones_row_f32"][:], dr["c_ones_row"]),
        p.dma("gpsimd", S["ident_bf"][:], dr["c_ident"]),
    ]
    for nm in ["bq", "bk", "bv", "bo", "bfq", "bfg"]:
        evs.append(p.dma("sync", S["l1_" + nm][:], dr["l1_" + nm]))
    p.wait("tensor", *evs); p.wait("vector", *evs); p.wait("scalar", *evs)
    emit_ln(p, S, dr["x"], dst_xln=S["XLN"])
    p.barrier()
    emit_l1_attn(p, S, S["XLN"], dr)
    p.barrier()
    emit_ln(p, S, dr["y_scr"], resid_dram=dr["x"], out_dram=dr["xo"])
    p.barrier()
    return p.build()


INV_A = 0.5875440658049354


def emit_l1_ffn(p, S, dr):
    PSA, PSB = S["PSA"], S["PSB"]
    pool = S["_pool"]
    pf = pool.bitcast(F32).ap()
    routed = pf[:, 0:16384].rearrange("p (a b) -> p a b", a=8)
    XLNH = S["f_xlnh"]
    A = S["f_A"]
    B1, B2 = S["f_B1"], S["f_B2"]
    F1, F2 = S["f_F1"], S["f_F2"]
    crep_sb = S["crep_sb"]
    crow_bf = S["f_crow"]
    cmbt_bf = S["f_cmbt_bf"]
    b1r, b3r = S["f_b1r"], S["f_b3r"]
    sb1r, sb3r = S["f_sb1r"], S["f_sb3r"]
    sb2r = S["f_sb2r"]
    ones512 = S["f_ones512"]
    wt0 = S["wt0"]
    idf = S["ident_f32"]
    orow = S["ones_row_f32"]
    V = "vector"

    ev_init = p.op_sig("vector", lambda v: v.memset(ones512[:], 1.0))
    ev_gw = p.dma("gpsimd", S["f_gwt"][:], dr["f_gwT"].rearrange("(k p) f -> p k f", p=128))
    ev_b2 = p.dma("gpsimd", S["f_b2sb"][:], dr["f_b2"])
    ev_sb = [p.dma("gpsimd", sb1r[:], dr["f_sb1"]),
             p.dma("gpsimd", sb3r[:], dr["f_sb3"]),
             p.dma("gpsimd", sb2r[:], dr["f_sb2"]),
             p.dma("sync", S["f_gbrow"][:], dr["f_gb"])]
    p.wait("tensor", ev_sb[3])
    p.op("tensor", lambda t: t.matmul(PSB[:, 0:8], orow[:], S["f_gbrow"][:], start=True, stop=True))
    ev_gbm = p.sig("tensor")
    p.wait("scalar", ev_gbm)
    ev_gbr = p.op_sig("scalar", lambda s: s.activation(S["f_gbr"][:], PSB[:, 0:8], AF.Copy))

    y_view = fm_view(dr["y_scr"])
    ev_rt_st = None
    ev_xl = None
    last_wt_use = None
    last_bias_use = None
    ev_cmbt_dma = None
    for th in range(2):
        tsl = slice(th * 2048, (th + 1) * 2048)
        p.wait("sync", S.get("_f_xlnh_free"))
        ev_xl = p.dma("sync", XLNH[:], dr["xln_scr"][:, :, tsl])

        scF8 = pf[0:8, 0:2048]
        cmbt_s = pf[0:8, 2048:4096]
        CMBf = pf[:, 4096:4224]
        svt8 = S["f_svt8"]
        psa_free = S.get("_f_psa")
        for tg in range(4):
            p.wait("tensor", ev_gw, ev_xl, psa_free)
            for k in range(8):
                p.op("tensor", lambda t, k=k, tg=tg: t.matmul(
                    PSA[0:8, 0:512], S["f_gwt"][:, k, :],
                    XLNH[:, k, tg * 512:(tg + 1) * 512], start=(k == 0), stop=(k == 7)))
            ev_mm = p.sig("tensor")
            p.wait("scalar", ev_mm, ev_rt_st)
            psa_free = p.op_sig("scalar", lambda s, tg=tg: s.activation(
                scF8[:, tg * 512:(tg + 1) * 512], PSA[0:8, 0:512], AF.Copy))
        ev_sc = psa_free
        psb_free = S.get("_f_psb")
        ev_cmb = None
        for bat in range(2):
            p.wait("tensor", ev_sc, psb_free)
            for j in range(8):
                tc = bat * 8 + j
                p.op("tensor", lambda t, j=j, tc=tc: t.matmul(
                    PSB[:, j * 8:(j + 1) * 8], scF8[:, tc * 128:(tc + 1) * 128],
                    idf[0:8, 0:8], start=True, stop=True))
            ev_t = p.sig("tensor")
            p.wait(V, ev_t)
            psb_free = p.opw(V, lambda v: v.tensor_copy(
                svt8[:].rearrange("p a b -> p (a b)"), PSB[:, 0:64]))
            sv = svt8[:].rearrange("p a b -> p a b")
            m1 = S["f_m1"][:].rearrange("p (a b) -> p a b", b=1)
            m1f = S["f_m1"]
            t8 = S["f_t8"][:].rearrange("p (a b) -> p a b", b=8)
            u8 = S["f_u8"][:].rearrange("p (a b) -> p a b", b=8)
            p.opw(V, lambda v, sv=sv, m1=m1: v.reduce_max(m1, sv, axis=AX.X, negate=True))
            p.opw(V, lambda v, sv=sv, t8=t8, m1=m1: v.tensor_tensor(
                t8, sv, m1.broadcast_to([128, 8, 8]), ALU.add))
            ev_sub = p.sig("vector")
            p.wait("scalar", ev_sub)
            ev_e = p.op_sig("scalar", lambda s, t8=t8, u8=u8: s.activation(
                u8.rearrange("p a b -> p (a b)"), t8.rearrange("p a b -> p (a b)"), AF.Exp))
            p.wait(V, ev_e)
            p.opw(V, lambda v, u8=u8, m1=m1: v.reduce_sum(m1, u8, axis=AX.X))
            p.opw(V, lambda v, m1f=m1f: v.reciprocal(S["f_m2"][:], m1f[:]))
            m2 = S["f_m2"][:].rearrange("p (a b) -> p a b", b=1)
            p.opw(V, lambda v, u8=u8, t8=t8, m2=m2: v.tensor_tensor(
                t8, u8, m2.broadcast_to([128, 8, 8]), ALU.mult))
            p.wait(V, ev_gbr)
            p.opw(V, lambda v, u8=u8, t8=t8: v.tensor_tensor(
                u8, t8, S["f_gbr"][:].rearrange("p (a b) -> p a b", a=1)
                .broadcast_to([128, 8, 8]), ALU.add))
            p.opw(V, lambda v, u8=u8, m1=m1: v.reduce_max(m1, u8, axis=AX.X))
            p.opw(V, lambda v, u8=u8, m1=m1: v.tensor_tensor(
                S["f_v8"][:].rearrange("p (a b) -> p a b", b=8), u8,
                m1.broadcast_to([128, 8, 8]), ALU.is_ge))
            v8 = S["f_v8"][:].rearrange("p (a b) -> p a b", b=8)
            p.opw(V, lambda v, u8=u8, v8=v8: v.scalar_tensor_tensor(
                S["f_w8"][:].rearrange("p (a b) -> p a b", b=8), v8, -1e30, u8,
                ALU.mult, ALU.add))
            w8 = S["f_w8"][:].rearrange("p (a b) -> p a b", b=8)
            p.opw(V, lambda v, w8=w8, m1=m1: v.reduce_max(m1, w8, axis=AX.X))
            p.opw(V, lambda v, u8=u8, m1=m1, v8=v8: v.tensor_tensor(
                v8, u8, m1.broadcast_to([128, 8, 8]), ALU.is_ge))
            ev_cmb = p.opw(V, lambda v, t8=t8, v8=v8, bat=bat: v.tensor_tensor(
                CMBf[:, bat * 64:(bat + 1) * 64].rearrange("p (a b) -> p a b", b=8),
                t8, v8, ALU.mult))
        psb_free2 = None
        for bat2 in range(4):
            p.wait("tensor", ev_cmb, psb_free2)
            for j in range(4):
                tc = bat2 * 4 + j
                p.op("tensor", lambda t, j=j, tc=tc: t.matmul(
                    PSB[0:8, j * 128:(j + 1) * 128], CMBf[:, tc * 8:(tc + 1) * 8],
                    idf[:], start=True, stop=True))
            ev_t2 = p.sig("tensor")
            p.wait(V, ev_t2)
            psb_free2 = p.opw(V, lambda v, bat2=bat2: v.tensor_copy(
                cmbt_s[:, bat2 * 512:(bat2 + 1) * 512], PSB[0:8, 0:512]))
        p.wait("sync", psb_free2)
        ev_cmbt_dma = p.dma("sync", dr["cmbt"][:, tsl], cmbt_s)
        S["_f_psb"] = psb_free2

        p.wait(V, ev_rt_st, ev_cmbt_dma)
        ev_z = p.op_sig("vector", lambda v: v.memset(
            routed[:].rearrange("p a b -> p (a b)"), 0.0))
        ev_acc = ev_z
        p.wait("gpsimd", ev_cmbt_dma, S.get("_f_cmbtbf"))
        ev_cbf = p.dma("gpsimd", cmbt_bf[:], dr["cmbt"][:, tsl])
        psa_free = S.get("_f_psa")
        psb_free = S["_f_psb"]
        ev_hp = None
        ev_a = None
        for e in range(9):
            shared = (e == 8)
            if not shared:
                p.wait("gpsimd", S.get("_f_crm"))
                ev_cr = p.dma("gpsimd", crow_bf[:], dr["cmbt"][e:e + 1, tsl])
                p.wait("tensor", ev_cr, psb_free)
                for ns in range(4):
                    p.op("tensor", lambda t, ns=ns: t.matmul(
                        PSB[:, ns * 512:(ns + 1) * 512], S["ones_row_bf"][:],
                        crow_bf[0:1, ns * 512:(ns + 1) * 512], start=True, stop=True))
                ev_crm = p.sig("tensor")
                S["_f_crm"] = ev_crm
                p.wait("scalar", ev_crm, ev_a)
                ev_cre = p.op_sig("scalar", lambda s: s.activation(
                    crep_sb[:], PSB[:, 0:2048], AF.Copy))
                psb_free = ev_cre
                p.wait("gpsimd", last_bias_use)
                ev_bb = [p.dma("gpsimd", b1r[:], dr["f_b1"][e:e + 1, :]),
                         p.dma("gpsimd", b3r[:], dr["f_b3"][e:e + 1, :])]
                w1_src, w3_src, w2_src = dr["f_w1T"][e], dr["f_w3T"][e], dr["f_w2T"][e]
                bb1, bb3 = b1r, b3r
            else:
                ev_bb = ev_sb[0:2]
                w1_src, w3_src, w2_src = dr["f_sw1T"], dr["f_sw3T"], dr["f_sw2T"]
                bb1, bb3 = sb1r, sb3r

            for fcg in range(8):
                specs = [(w1_src, fcg, bb1), (w1_src, 8 + fcg, bb1),
                         (w3_src, fcg, bb3), (w3_src, 8 + fcg, bb3)]
                outs = []
                for gi, (wsrc, mc, brow) in enumerate(specs):
                    p.wait("gpsimd", last_wt_use)
                    ev_w = p.dma("gpsimd", wt0[:],
                                 wsrc[:, mc * 128:(mc + 1) * 128].rearrange("(k p) f -> p k f", p=128))
                    p.wait("tensor", ev_w, ev_xl, psa_free, ev_bb[0], ev_bb[1], ev_init)
                    for k in range(8):
                        for ns in range(4):
                            p.op("tensor", lambda t, k=k, ns=ns, th=th: t.matmul(
                                PSA[:, ns * 512:(ns + 1) * 512], wt0[:, k, :],
                                XLNH[:, k, ns * 512:(ns + 1) * 512],
                                start=(k == 0), stop=False))
                    for ns in range(4):
                        p.op("tensor", lambda t, ns=ns, mc=mc, brow=brow: t.matmul(
                            PSA[:, ns * 512:(ns + 1) * 512], brow[0:1, mc * 128:(mc + 1) * 128],
                            ones512[:], start=False, stop=True))
                    ev_mm = p.sig("tensor")
                    last_wt_use = ev_mm
                    if gi < 2:
                        dstb = B1 if gi == 0 else B2
                        p.wait("scalar", ev_mm, ev_hp)
                        psa_free = p.op_sig("scalar", lambda s, dstb=dstb: s.activation(
                            dstb[:], PSA[:, 0:2048], AF.Copy))
                    else:
                        srcb, dstf = (B1, F1) if gi == 2 else (B2, F2)
                        p.wait(V, ev_mm, ev_a if gi == 2 else None)
                        psa_free = p.opw(V, lambda v, srcb=srcb, dstf=dstf: v.tensor_mul(
                            dstf[:], srcb[:], PSA[:, 0:2048]))
                        ev_hp = psa_free
                p.opw(V, lambda v: v.tensor_scalar_min(B1[:], F1[:], 7.0))
                ev_gm = p.sig("vector")
                p.wait("scalar", ev_gm)
                ev_sg = p.op_sig("scalar", lambda s: s.activation(F1[:], B1[:], AF.Silu, scale=1.702))
                p.opw(V, lambda v: v.tensor_scalar(B2[:], F2[:], 7.0, -7.0, ALU.min, ALU.max))
                p.opw(V, lambda v: v.tensor_scalar(F2[:], B2[:], INV_A, INV_A, ALU.mult, ALU.add))
                p.wait(V, ev_sg)
                p.opw(V, lambda v: v.tensor_mul(B1[:], F1[:], F2[:]))
                if shared:
                    ev_a = p.opw(V, lambda v, fcg=fcg: v.tensor_copy(A[:, fcg, :], B1[:]))
                else:
                    p.wait(V, ev_cre)
                    ev_a = p.opw(V, lambda v, fcg=fcg: v.tensor_mul(A[:, fcg, :], B1[:], crep_sb[:]))
            for hc in range(8):
                p.wait("gpsimd", last_wt_use)
                ev_w2 = p.dma("gpsimd", wt0[:],
                              w2_src[:, hc * 128:(hc + 1) * 128].rearrange("(k p) f -> p k f", p=128))
                p.wait("tensor", ev_w2, psa_free, ev_a)
                for k in range(8):
                    for ns in range(4):
                        p.op("tensor", lambda t, k=k, ns=ns: t.matmul(
                            PSA[:, ns * 512:(ns + 1) * 512], wt0[:, k, :],
                            A[:, k, ns * 512:(ns + 1) * 512], start=(k == 0),
                            stop=(False if shared else (k == 7))))
                if shared:
                    for ns in range(4):
                        p.op("tensor", lambda t, ns=ns, hc=hc: t.matmul(
                            PSA[:, ns * 512:(ns + 1) * 512], sb2r[0:1, hc * 128:(hc + 1) * 128],
                            ones512[:], start=False, stop=True))
                ev_y = p.sig("tensor")
                last_wt_use = ev_y
                p.wait(V, ev_y, ev_acc)
                ev_acc = p.opw(V, lambda v, hc=hc: v.tensor_add(
                    routed[:, hc, :], routed[:, hc, :], PSA[:, 0:2048]))
                psa_free = ev_acc
            if not shared:
                last_bias_use = last_wt_use
        for hc in range(8):
            p.wait("tensor", ev_cbf, ev_b2, psb_free)
            for ns in range(4):
                p.op("tensor", lambda t, ns=ns, hc=hc: t.matmul(
                    PSB[:, ns * 512:(ns + 1) * 512], S["f_b2sb"][:, hc * 128:(hc + 1) * 128],
                    cmbt_bf[:, ns * 512:(ns + 1) * 512], start=True, stop=True))
            ev_bm = p.sig("tensor")
            p.wait(V, ev_bm, ev_acc)
            ev_acc = p.opw(V, lambda v, hc=hc: v.tensor_add(
                routed[:, hc, :], routed[:, hc, :], PSB[:, 0:2048]))
            psb_free = ev_acc
        S["_f_cmbtbf"] = ev_bm
        S["_f_psa"] = psa_free
        S["_f_psb"] = psb_free
        p.wait("sync", ev_acc)
        for hc in range(8):
            ev_rt_st = p.dma("sync", y_view[:, hc, tsl], routed[:, hc, :])
        S["_f_xlnh_free"] = last_wt_use
    p.wait(V, ev_rt_st)
    return


def alloc_l1ffn_sbuf(nc):
    S = {}
    def sb(name, shape, dtype):
        S[name] = nc.alloc_sbuf_tensor(name, shape, dtype).ap()
        return S[name]
    pool = nc.alloc_sbuf_tensor("POOL", [128, 33280], BF16)
    S["_pool"] = pool
    pf = pool.bitcast(F32).ap()
    pb = pool.ap()
    S["ln_xc"] = pf[:, 0:4096].rearrange("p (a b) -> p a b", a=8)
    S["ln_rc"] = pf[:, 4096:8192].rearrange("p (a b) -> p a b", a=8)
    S["ln_xb"] = pb[:, 16384:20480].rearrange("p (a b) -> p a b", a=8)
    S["ln_sq"] = pb[:, 20480:24576].rearrange("p (a b) -> p a b", a=8)
    S["ln_t"] = pf[:, 8192:12288].rearrange("p (a b) -> p a b", a=8)
    S["ln_rows"] = pf[0:1, 12288:15360]
    S["ln_reps"] = pf[:, 15360:16384]
    sb("f_xlnh", [128, 8, 2048], BF16)
    sb("f_A", [128, 8, 2048], BF16)
    sb("f_B1", [128, 2048], BF16)
    sb("f_B2", [128, 2048], BF16)
    sb("f_F1", [128, 2048], F32)
    sb("f_F2", [128, 2048], F32)
    sb("crep_sb", [128, 2048], BF16)
    sb("f_crow", [1, 2048], BF16)
    sb("f_cmbt_bf", [8, 2048], BF16)
    sb("f_b1r", [1, 2048], BF16)
    sb("f_b3r", [1, 2048], BF16)
    sb("f_sb1r", [1, 2048], BF16)
    sb("f_sb3r", [1, 2048], BF16)
    sb("f_sb2r", [1, 1024], BF16)
    sb("f_ones512", [1, 512], BF16)
    sb("f_b2sb", [8, 1024], BF16)
    sb("f_gwt", [128, 8, 8], BF16)
    sb("f_gbrow", [1, 8], F32)
    sb("f_gbr", [128, 8], F32)
    sb("f_svt8", [128, 8, 8], F32)
    sb("f_m1", [128, 8], F32)
    sb("f_m2", [128, 8], F32)
    sb("f_t8", [128, 64], F32)
    sb("f_u8", [128, 64], F32)
    sb("f_v8", [128, 64], F32)
    sb("f_w8", [128, 64], F32)
    sb("wt0", [128, 8, 128], BF16)
    sb("wt1", [128, 8, 128], BF16)
    sb("ones_row_f32", [1, 128], F32)
    sb("ones_row_bf", [1, 128], BF16)
    sb("ones_col_bf", [128, 1], BF16)
    sb("ident_f32", [128, 128], F32)
    S["PSA"] = nc.alloc_psum_tensor("PSA", [128, 2048], F32).ap()
    S["PSB"] = nc.alloc_psum_tensor("PSB", [128, 2048], F32).ap()
    return S


def build_l1ffn_block(debug=False):
    p = Prog()
    nc = p.nc
    dr = {}
    dr["x"] = nc.dram_tensor("x", [D, T], F32, kind="ExternalInput").ap()
    dr["f_w1T"] = nc.dram_tensor("f_w1T", [8, 1024, 2048], F32, kind="ExternalInput").ap()
    dr["f_w3T"] = nc.dram_tensor("f_w3T", [8, 1024, 2048], F32, kind="ExternalInput").ap()
    dr["f_w2T"] = nc.dram_tensor("f_w2T", [8, 1024, 1024], F32, kind="ExternalInput").ap()
    dr["f_b1"] = nc.dram_tensor("f_b1", [8, 2048], F32, kind="ExternalInput").ap()
    dr["f_b3"] = nc.dram_tensor("f_b3", [8, 2048], F32, kind="ExternalInput").ap()
    dr["f_b2"] = nc.dram_tensor("f_b2", [8, 1024], F32, kind="ExternalInput").ap()
    dr["f_sw1T"] = nc.dram_tensor("f_sw1T", [1024, 2048], F32, kind="ExternalInput").ap()
    dr["f_sw3T"] = nc.dram_tensor("f_sw3T", [1024, 2048], F32, kind="ExternalInput").ap()
    dr["f_sw2T"] = nc.dram_tensor("f_sw2T", [1024, 1024], F32, kind="ExternalInput").ap()
    dr["f_sb1"] = nc.dram_tensor("f_sb1", [1, 2048], F32, kind="ExternalInput").ap()
    dr["f_sb3"] = nc.dram_tensor("f_sb3", [1, 2048], F32, kind="ExternalInput").ap()
    dr["f_sb2"] = nc.dram_tensor("f_sb2", [1, 1024], F32, kind="ExternalInput").ap()
    dr["f_gwT"] = nc.dram_tensor("f_gwT", [1024, 8], F32, kind="ExternalInput").ap()
    dr["f_gb"] = nc.dram_tensor("f_gb", [1, 8], F32, kind="ExternalInput").ap()
    dr["c_ones_row"] = nc.dram_tensor("c_ones_row", [1, 128], F32, kind="ExternalInput").ap()
    dr["c_ones_col"] = nc.dram_tensor("c_ones_col", [128, 1], F32, kind="ExternalInput").ap()
    dr["c_ident_f32"] = nc.dram_tensor("c_ident_f32", [128, 128], F32, kind="ExternalInput").ap()
    kind_dbg = "ExternalOutput" if debug else "Internal"
    dr["xln_scr"] = nc.dram_tensor("xln_scr", [D, T], BF16, kind="Internal").ap().rearrange(
        "(f p) t -> p f t", p=128)
    dr["cmbt"] = nc.dram_tensor("cmbt", [8, T], F32, kind=kind_dbg).ap()
    dr["y_scr"] = nc.dram_tensor("y_scr", [D, T], F32, kind=kind_dbg).ap()
    dr["xo"] = nc.dram_tensor("xo", [D, T], F32, kind="ExternalOutput").ap()

    S = alloc_l1ffn_sbuf(nc)
    evs = [
        p.dma("sync", S["ones_row_f32"][:], dr["c_ones_row"]),
        p.dma("gpsimd", S["ones_row_bf"][:], dr["c_ones_row"]),
        p.dma("gpsimd", S["ones_col_bf"][:], dr["c_ones_col"]),
        p.dma("sync", S["ident_f32"][:], dr["c_ident_f32"]),
    ]
    p.wait("tensor", *evs); p.wait("vector", *evs); p.wait("scalar", *evs)
    emit_ln(p, S, dr["x"], dst_xln_dram=dr["xln_scr"])
    p.barrier()
    emit_l1_ffn(p, S, dr)
    p.barrier()
    emit_ln(p, S, dr["y_scr"], resid_dram=dr["x"], out_dram=dr["xo"])
    p.barrier()
    return p.build()


def alloc_l0attn_sbuf(nc):
    S = {}
    def sb(name, shape, dtype):
        S[name] = nc.alloc_sbuf_tensor(name, shape, dtype).ap()
        return S[name]
    sb("XLN", [128, 8, T], BF16)
    pool = nc.alloc_sbuf_tensor("POOL", [128, 33280], BF16)
    S["VT"] = pool.ap()[:, 0:32768].rearrange("p (tc pc f) -> p tc pc f", tc=32, pc=8)
    S["CTX"] = pool.ap()[:, 0:32768].rearrange("p (f t) -> p f t", f=8)
    pf = pool.bitcast(F32).ap()
    pb = pool.ap()
    S["ln_xc"] = pf[:, 0:4096].rearrange("p (a b) -> p a b", a=8)
    S["ln_rc"] = pf[:, 4096:8192].rearrange("p (a b) -> p a b", a=8)
    S["ln_xb"] = pb[:, 16384:20480].rearrange("p (a b) -> p a b", a=8)
    S["ln_sq"] = pb[:, 20480:24576].rearrange("p (a b) -> p a b", a=8)
    S["ln_t"] = pf[:, 8192:12288].rearrange("p (a b) -> p a b", a=8)
    S["ln_rows"] = pf[0:1, 12288:15360]
    S["ln_reps"] = pf[:, 15360:16384]
    sb("wt0", [128, 8, 128], BF16)
    sb("wt1", [128, 8, 128], BF16)
    sb("dn_aug", [65, T], BF16)
    sb("kn_aug", [65, T], BF16)
    sb("QP", [128, 2048], BF16)
    sb("KP", [128, 2048], BF16)
    sb("QPF", [64, T], BF16)
    sb("kv_sb", [64, 65], BF16)
    sb("sqt", [64, 2048], BF16)
    sb("subt", [128, 512], F32)
    sb("etmp", [128, 512], F32)
    sb("mx", [128, 8], F32)
    sb("nrow", [1, 2048], F32)
    sb("ctx_f", [64, 2048], F32)
    sb("ctx_hb", [64, 2048], BF16)
    sb("vtmp", [128, 2048], BF16)
    sb("ytmp", [128, 2048], F32)
    S["nrow_r"] = S["ytmp"][0:1, 0:2048]
    sb("ones_col_bf", [128, 1], BF16)
    sb("ones_row_f32", [1, 128], F32)
    sb("neghalf_bf", [64, 1], BF16)
    sb("ident_bf", [128, 128], BF16)
    sb("pm_bf", [65, 64], BF16)
    S["b_qkv"] = nc.alloc_sbuf_tensor("sb_b_qkv", [128, 24], F32).ap()
    S["b_qkv8"] = nc.alloc_sbuf_tensor("sb_b_qkv8", [128, 16], F32).ap()
    S["b_proj"] = nc.alloc_sbuf_tensor("sb_b_proj", [128, 8], F32).ap()
    S["PSA"] = nc.alloc_psum_tensor("PSA", [128, 2048], F32).ap()
    S["PSB"] = nc.alloc_psum_tensor("PSB", [128, 2048], F32).ap()
    return S


def emit_l0attn_consts(p, S, dr):
    evs = [
        p.dma("gpsimd", S["ones_col_bf"][:], dr["c_ones_col"]),
        p.dma("gpsimd", S["neghalf_bf"][:], dr["c_neghalf_col"]),
        p.dma("gpsimd", S["ident_bf"][:], dr["c_ident"]),
        p.dma("sync", S["ones_row_f32"][:], dr["c_ones_row"]),
        p.dma("sync", S["b_qkv"][:], dr["b_qkv"]),
        p.dma("sync", S["b_proj"][:], dr["b_proj"]),
    ]
    p.wait("vector", *evs)
    ev = p.op_sig("vector", lambda v: v.tensor_scalar_mul(
        S["b_qkv8"][:], S["b_qkv"][:, 0:16], 0.125))
    p.wait("tensor", *evs)
    p.wait("scalar", *evs, ev)
    p.wait("tensor", ev)
    return


def build_l0attn_block(debug=False):
    p = Prog()
    nc = p.nc
    dr = {}
    dr["x"] = nc.dram_tensor("x", [D, T], F32, kind="ExternalInput").ap()
    dr["w_qkvT"] = nc.dram_tensor("w_qkvT", [1024, 3072], F32, kind="ExternalInput").ap()
    dr["b_qkv"] = nc.dram_tensor("b_qkv", [128, 24], F32, kind="ExternalInput").ap()
    dr["w_projT"] = nc.dram_tensor("w_projT", [1024, 1024], F32, kind="ExternalInput").ap()
    dr["b_proj"] = nc.dram_tensor("b_proj", [128, 8], F32, kind="ExternalInput").ap()
    dr["pm_rhs"] = nc.dram_tensor("pm_rhs", [65, 64], F32, kind="ExternalInput").ap()
    dr["c_ones_col"] = nc.dram_tensor("c_ones_col", [128, 1], F32, kind="ExternalInput").ap()
    dr["c_ones_row"] = nc.dram_tensor("c_ones_row", [1, 128], F32, kind="ExternalInput").ap()
    dr["c_neghalf_col"] = nc.dram_tensor("c_neghalf_col", [64, 1], F32, kind="ExternalInput").ap()
    dr["c_ident"] = nc.dram_tensor("c_ident", [128, 128], F32, kind="ExternalInput").ap()
    kind_dbg = "ExternalOutput" if debug else "Internal"
    dr["ctx_scr"] = nc.dram_tensor("ctx_scr", [D, T], BF16, kind=kind_dbg).ap()
    dr["y_scr"] = nc.dram_tensor("y_scr", [D, T], F32, kind=kind_dbg).ap()
    if debug:
        dr["dbg_dn"] = nc.dram_tensor("dbg_dn", [65, T], BF16, kind="ExternalOutput").ap()
        dr["dbg_kn"] = nc.dram_tensor("dbg_kn", [65, T], BF16, kind="ExternalOutput").ap()
        dr["dbg_qp"] = nc.dram_tensor("dbg_qp", [128, 2048], BF16, kind="ExternalOutput").ap()
        dr["dbg_kp"] = nc.dram_tensor("dbg_kp", [128, 2048], BF16, kind="ExternalOutput").ap()
        dr["dbg_kv"] = nc.dram_tensor("dbg_kv", [64, 65], BF16, kind="ExternalOutput").ap()
        dr["dbg_qpf"] = nc.dram_tensor("dbg_qpf", [64, T], BF16, kind="ExternalOutput").ap()
    dr["xo"] = nc.dram_tensor("xo", [D, T], F32, kind="ExternalOutput").ap()

    S = alloc_l0attn_sbuf(nc)
    emit_l0attn_consts(p, S, dr)
    xln = S["XLN"]
    emit_ln(p, S, dr["x"], dst_xln=xln)
    p.barrier()
    emit_l0_attn(p, S, xln, dr)
    p.barrier()
    emit_ln(p, S, dr["y_scr"], resid_dram=dr["x"], out_dram=dr["xo"])
    p.barrier()
    return p.build()


def build_ln_test():
    p = Prog()
    nc = p.nc
    x_dram = nc.dram_tensor("x", [D, T], F32, kind="ExternalInput").ap()
    r_dram = nc.dram_tensor("r", [D, T], F32, kind="ExternalInput").ap()
    y_dram = nc.dram_tensor("y", [D, T], BF16, kind="ExternalOutput").ap()
    z_dram = nc.dram_tensor("z", [D, T], F32, kind="ExternalOutput").ap()
    ones_col_f = nc.dram_tensor("ones_col", [128, 1], F32, kind="ExternalInput").ap()
    ones_row_f = nc.dram_tensor("ones_row", [1, 128], F32, kind="ExternalInput").ap()

    S = {}
    S["ln_xc"] = nc.alloc_sbuf_tensor("ln_xc", [128, 8, 512], F32).ap()
    S["ln_rc"] = nc.alloc_sbuf_tensor("ln_rc", [128, 8, 512], F32).ap()
    S["ln_xb"] = nc.alloc_sbuf_tensor("ln_xb", [128, 8, 512], BF16).ap()
    S["ln_sq"] = nc.alloc_sbuf_tensor("ln_sq", [128, 8, 512], BF16).ap()
    S["ln_rows"] = nc.alloc_sbuf_tensor("ln_rows", [1, 3072], F32).ap()
    S["ln_t"] = nc.alloc_sbuf_tensor("ln_t", [128, 8, 512], F32).ap()
    S["ln_reps"] = nc.alloc_sbuf_tensor("ln_reps", [128, 1024], F32).ap()
    S["PSA"] = nc.alloc_psum_tensor("PSA", [128, 2048], F32).ap()
    S["PSB"] = nc.alloc_psum_tensor("PSB", [128, 2048], F32).ap()
    S["ones_col_bf"] = nc.alloc_sbuf_tensor("ones_col_bf", [128, 1], BF16).ap()
    S["ones_row_f32"] = nc.alloc_sbuf_tensor("ones_row_f32", [1, 128], F32).ap()
    xln = nc.alloc_sbuf_tensor("xln", [128, 8, T], BF16).ap()

    ev1 = p.dma("gpsimd", S["ones_col_bf"][:], ones_col_f)
    ev2 = p.dma("sync", S["ones_row_f32"][:], ones_row_f)
    p.wait("tensor", ev1, ev2)
    p.wait("vector", ev1, ev2)

    emit_ln(p, S, x_dram, dst_xln=xln)
    p.barrier()
    emit_ln(p, S, x_dram, resid_dram=r_dram, out_dram=z_dram)
    p.barrier()
    ev = p.dma("sync", y_dram.rearrange("(f p) t -> p f t", p=128), xln[:])
    p.wait("sync", ev)
    return p.build()


L0ATTN_KEYS = ["w_qkvT", "b_qkv", "w_projT", "b_proj", "pm_rhs",
               "c_ones_col", "c_ones_row", "c_neghalf_col", "c_ident"]
L0FFN_KEYS = ["w_gateT", "w1eT", "w3eT", "w2eT", "w1sT", "w3sT", "w2sT",
              "c_ones_col", "c_ones_row", "c_ident_f32"]
L1ATTN_KEYS = (["l1_" + n for n in ["wq", "wk", "wv", "wo", "wfq", "wfg",
                                    "bq", "bk", "bv", "bo", "bfq", "bfg"]] +
               ["l1_uvdiag", "c_iota", "c_ones_col", "c_ones_row", "c_ident"])
L1FFN_KEYS = ["f_w1T", "f_w3T", "f_w2T", "f_b1", "f_b3", "f_b2", "f_sw1T", "f_sw3T",
              "f_sw2T", "f_sb1", "f_sb3", "f_sb2", "f_gwT", "f_gb",
              "c_ones_row", "c_ones_col", "c_ident_f32"]


def kernel(**inputs):
    g = host_prep(inputs)
    x = np.asarray(inputs["L0_attn_x"])
    xs = [np.ascontiguousarray(x[b].T) for b in range(8)]
    blocks = [
        (build_l0attn_block(), L0ATTN_KEYS),
        (build_l0ffn_block(), L0FFN_KEYS),
        (build_l1attn_block(), L1ATTN_KEYS),
        (build_l1ffn_block(), L1FFN_KEYS),
    ]
    for nc, keys in blocks:
        in_maps = []
        for b in range(8):
            m = {"x": xs[b]}
            m.update({k: g[k] for k in keys})
            in_maps.append(m)
        for attempt in range(3):
            res = run_bass_kernel_spmd(nc, in_maps, core_ids=list(range(8)))
            nxt = [np.ascontiguousarray(np.asarray(res.results[b]["xo"])) for b in range(8)]
            if all(np.isfinite(a).all() for a in nxt):
                break
        xs = nxt
    out = np.stack([xs[b].T for b in range(8)], axis=0).astype(np.float32)
    return out
